# version 01
# baseline (speedup 1.0000x reference)
import os
import numpy as np
import ml_dtypes
from contextlib import ExitStack

import concourse.bass as bass
import concourse.mybir as mybir
from concourse.bass_utils import run_bass_kernel_spmd

F32 = mybir.dt.float32
BF16 = mybir.dt.bfloat16
U32 = mybir.dt.uint32
U8 = mybir.dt.uint8
AF = mybir.ActivationFunctionType
ALU = mybir.AluOpType
AX = mybir.AxisListType

T = 2048
D = 4096
NCH = 32
NG = 2
TG = 1024
ISQ = float(1.0 / np.sqrt(128.0))
EPS = 1e-5

ENGS = ("pe", "vec", "scl", "gps", "sync")


class Mile:
    __slots__ = ("sem", "val", "eng")

    def __init__(self, sem, val, eng):
        self.sem, self.val, self.eng = sem, val, eng


class Plan:
    def __init__(self, nc, ctx):
        self.nc = nc
        self.ctx = ctx
        self.prog = {e: [] for e in ENGS}
        self.semh = {}
        self.cnt = {}

    def sem(self, name):
        if name not in self.semh:
            self.semh[name] = self.ctx.enter_context(self.nc.semaphore(name))
            self.cnt[name] = 0
        return name

    def emit(self, eng, fn, waits=(), inc=None, amt=1, hard_waits=()):
        ws = []
        for m in waits:
            if m is None or m.eng == eng:
                continue
            ws.append((m.sem, m.val))
        for m in hard_waits:
            if m is None:
                continue
            ws.append((m.sem, m.val))
        if inc is not None:
            self.sem(inc)
            self.cnt[inc] += amt
            rec = (inc, self.cnt[inc], amt)
            mile = Mile(inc, self.cnt[inc], eng if amt == 1 else None)
        else:
            rec, mile = None, None
        self.prog[eng].append((fn, tuple(ws), rec))
        return mile

    def now(self, semname, eng=None):
        return Mile(semname, self.cnt.get(semname, 0), eng)

    def barrier_wait(self, eng, miles):
        ws = [m for m in miles if m is not None and m.eng != eng]
        if not ws:
            return
        self.prog[eng].append((lambda e: e.nop(), tuple((m.sem, m.val) for m in ws), None))

    def check_deadlock(self):
        heads = {e: 0 for e in ENGS}
        counts = {}
        progressed = True
        while progressed:
            progressed = False
            for e in ENGS:
                while heads[e] < len(self.prog[e]):
                    fn, ws, rec = self.prog[e][heads[e]]
                    if all(counts.get(sm, 0) >= v for sm, v in ws):
                        if rec is not None:
                            sm, v, a = rec
                            counts[sm] = counts.get(sm, 0) + a
                        heads[e] += 1
                        progressed = True
                    else:
                        break
        stuck = {e: heads[e] for e in ENGS if heads[e] < len(self.prog[e])}
        if stuck:
            msg = []
            for e, h in stuck.items():
                fn, ws, rec = self.prog[e][h]
                missing = [(sm, v, counts.get(sm, 0)) for sm, v in ws
                           if counts.get(sm, 0) < v]
                msg.append(f"{e}[{h}/{len(self.prog[e])}] blocked on {missing[:4]}")
            raise RuntimeError("DEADLOCK:\n" + "\n".join(msg))

    def replay_into(self, block):
        def mk(eng):
            def run(e):
                seen = {}
                for fn, ws, rec in self.prog[eng]:
                    for s, v in ws:
                        if seen.get(s, -1) < v:
                            e.wait_ge(self.semh[s], v)
                            seen[s] = v
                    inst = fn(e)
                    if rec is not None:
                        s, v, a = rec
                        inst.then_inc(self.semh[s], a)
            return run

        block.tensor(mk("pe"))
        block.vector(mk("vec"))
        block.scalar(mk("scl"))
        block.gpsimd(mk("gps"))
        block.sync(mk("sync"))


class Stream:
    def __init__(self, plan, name, eng, nbuf):
        self.plan = plan
        self.eng = eng
        self.nbuf = nbuf
        self.sems = [plan.sem(f"{name}{i}") for i in range(nbuf)]
        self.i = 0

    def load(self, fn, waits=()):
        b = self.i % self.nbuf
        m = self.plan.emit(self.eng, fn, waits=waits, inc=self.sems[b], amt=16)
        self.i += 1
        return b, Mile(m.sem, m.val, None)

    def final_miles(self):
        return [Mile(s, self.plan.cnt[s], None) for s in self.sems]


def build_program(n_sub=4, debug=False):
    nc = bass.Bass("TRN2", target_bir_lowering=False)
    ctx = ExitStack()
    p = Plan(nc, ctx)

    def din(name, shape, dt):
        return nc.dram_tensor(name, shape, dt, kind="ExternalInput").ap()

    XIN = din("xin", [NG, D, TG], F32)
    WQKV_A = din("wqkv_a", [24, 128, NCH, 256], F32)
    WO_A = din("wo_a", [16, 128, NCH, 256], F32)
    WQ_B = din("wq_b", [8, 128, NCH, 4, 128], F32)
    WKV_B = din("wkv_b", [8, 128, NCH, 256], F32)
    WO_B = din("wo_b", [16, 128, 8, 256], F32)
    WUP = din("wup", [16, 128, 2, 1024], F32)
    WDN = din("wdn", [16, 128, 8, 256], F32)
    PERMC = din("permc", [128, 32], U32)
    CSQ = din("csq", [2, 128, T], F32)
    CSK = din("csk", [2, 128, T], F32)
    MASKC = din("maskc", [128, 128], F32)
    IDENB = din("idenb", [128, 128], BF16)
    ONESCB = din("onescb", [128, 1], BF16)
    ONESR = din("onesr", [1, 128], F32)
    EPSC = din("epsc", [128, 1], F32)
    BIDX = din("bidx", [128, 2], U32)

    XOUT = nc.dram_tensor("xout", [NG, D, TG], F32, kind="ExternalOutput").ap()
    if debug:
        D_ROWS = nc.dram_tensor("d_rows", [2, 128, T], F32, kind="ExternalOutput").ap()
        D_KT = nc.dram_tensor("d_kt", [128, 8 * T], BF16, kind="ExternalOutput").ap()
        D_VT = nc.dram_tensor("d_vt", [128, 8 * T], BF16, kind="ExternalOutput").ap()
        D_QTS = nc.dram_tensor("d_qts", [NG, 32, 128, TG], BF16, kind="ExternalOutput").ap()
        D_OBLK = nc.dram_tensor("d_oblk", [128, 32768], BF16, kind="ExternalOutput").ap()
        D_PB = nc.dram_tensor("d_pb", [128, T], BF16, kind="ExternalOutput").ap()
        D_PTB = nc.dram_tensor("d_ptb", [128, 8192], BF16, kind="ExternalOutput").ap()
        D_COLS = nc.dram_tensor("d_cols", [128, 16], F32, kind="ExternalOutput").ap()
        dbg_st = []
        dbg_att = {"pb": D_PB, "ptb": D_PTB, "cols": D_COLS, "st": dbg_st}
    XA = nc.dram_tensor("XA", [NG, D, TG], F32, kind="Internal").ap()
    XB = nc.dram_tensor("XB", [NG, D, TG], F32, kind="Internal").ap()
    XNS = nc.dram_tensor("XNS", [D, T], BF16, kind="Internal").ap()
    RROWS = nc.dram_tensor("RROWS", [2, T], F32, kind="Internal").ap()
    QTS = nc.dram_tensor("QTS", [NG, 32, 128, TG], BF16, kind="Internal").ap()

    sb = lambda n, s, d: ctx.enter_context(nc.sbuf_tensor(n, s, d))
    rows1 = sb("rows1", [128, T], F32)
    rows2 = sb("rows2", [128, T], F32)
    rmr = sb("rmr", [128, 2048], F32)
    wbuf_u8 = sb("wbuf", [128, 3 * 4096], U8)
    WG = [wbuf_u8[:, i * 4096:(i + 1) * 4096].bitcast(BF16) for i in range(3)]
    maskb = sb("maskb", [128, 128], F32)
    idenb = sb("idenb_s", [128, 128], BF16)
    onescb = sb("onescb_s", [128, 1], BF16)
    onesrb = sb("onesr_s", [1, 128], F32)
    epscb = sb("epsc_s", [128, 1], F32)
    permb = sb("permb", [128, 32], U32)
    bidxb = sb("bidxb", [128, 2], U32)
    colsb = sb("colsb", [128, 16], F32)
    vtmp = sb("vtmp", [128, 1024], BF16)
    pbuf = sb("pbuf", [128, T], BF16)
    bigB_u8 = sb("bigB", [128, 65536], U8)
    KT = sb("KT", [128, 8 * T], BF16)
    VT = sb("VT", [128, 8 * T], BF16)
    qst = sb("qst", [128, 2048], BF16)
    qld = sb("qld", [128, 2048], BF16)
    bigD_u8 = sb("bigD", [128, 24576], U8)

    xn_sb = bigB_u8[:, 0:65536].bitcast(BF16)
    oblk = xn_sb
    ug_sb = bigB_u8[:, 0:32768].bitcast(BF16)
    xg_sb = [bigB_u8[:, 32768 + i * 8192:32768 + (i + 1) * 8192].bitcast(BF16) for i in range(2)]
    xnew_ffn = bigB_u8[:, 49152:65536].bitcast(F32)

    xld = [bigD_u8[:, i * 4096:(i + 1) * 4096].bitcast(F32) for i in range(2)]
    lnscr = bigD_u8[:, 8192:12288].bitcast(F32)
    sqb = bigD_u8[:, 12288:14336].bitcast(BF16)
    xbfb = bigD_u8[:, 14336:16384].bitcast(BF16)
    csb = bigD_u8[:, 12288:20480].bitcast(F32)
    qtmp = bigD_u8[:, 20480:22528].bitcast(F32)
    qswp = bigD_u8[:, 22528:24576].bitcast(F32)
    r3row = bigD_u8[:, 16384:24576].bitcast(F32)[0:1, :]
    ptb = bigD_u8[:, 0:16384].bitcast(BF16)
    xoldb = [bigD_u8[:, i * 8192:(i + 1) * 8192].bitcast(F32) for i in range(2)]
    xnstb = bigD_u8[:, 16384:24576].bitcast(F32)
    xresb = bigD_u8[:, 0:16384].bitcast(F32)
    xnstg = [bigB_u8[:, 49152 + i * 2048:49152 + (i + 1) * 2048].bitcast(BF16) for i in range(2)]

    PS = ctx.enter_context(nc.psum_tensor("PS", [128, 4096], F32))
    PSbf = PS[:, :].bitcast(BF16)

    def bank(i, w=512):
        return PS[:, i * 512:i * 512 + w]

    def bankbf(i, off=0, w=1024):
        return PSbf[:, i * 1024 + off:i * 1024 + off + w]

    wst = Stream(p, "qw", "gps", 3)
    xst = Stream(p, "qx", "sync", 2)
    qsx = Stream(p, "qq", "sync", 2)
    oldst = Stream(p, "qo", "sync", 2)
    gxg = Stream(p, "gg", "gps", 2)
    grs = Stream(p, "gr", "gps", 1)
    stq = Stream(p, "sq", "sync", 2)
    stn = Stream(p, "sn", "sync", 2)
    stx = Stream(p, "sx", "sync", 2)
    stsc = Stream(p, "sc", "gps", 2)
    mld = Stream(p, "ml", "sync", 1)
    qsw = Stream(p, "qv", "sync", 2)
    rrw = Stream(p, "rw", "sync", 2)
    rbc = Stream(p, "rb", "gps", 2)

    wfree = {}
    rmr_guard = [None]
    qldfree = {}

    MPE, MVEC, MSCL = "mpe", "mvec", "mscl"
    for s in (MPE, MVEC, MSCL):
        p.sem(s)

    mld.load(lambda e: e.dma_start(out=maskb[:], in_=MASKC))
    mld.load(lambda e: e.dma_start(out=idenb[:], in_=IDENB))
    mld.load(lambda e: e.dma_start(out=onescb[:], in_=ONESCB))
    mld.load(lambda e: e.dma_start(out=onesrb[:], in_=ONESR))
    mld.load(lambda e: e.dma_start(out=epscb[:], in_=EPSC))
    mld.load(lambda e: e.dma_start(out=bidxb[:], in_=BIDX))
    _, consts_ready = mld.load(lambda e: e.dma_start(out=permb[:], in_=PERMC))

    def stats_pass(src, deps=()):
        r1 = rows1[0:1, :]
        r2 = rows2[0:1, :]
        r3 = r3row
        pe_mile = None
        buf_guard = {}
        dep_list = list(deps) + [consts_ready]
        for g in range(NG):
            for cc in range(NCH):
                par, lm = xst.load(
                    lambda e, g=g, cc=cc, b=xst.i % 2: e.dma_start(
                        out=xld[b][:], in_=src[g, cc * 128:(cc + 1) * 128, :]),
                    waits=dep_list + [buf_guard.get(xst.i % 2)])
                sm = p.emit("scl", lambda e, par=par: e.activation(
                    sqb[:], xld[par][:], AF.Square), waits=[lm, pe_mile], inc=MSCL)
                vm = p.emit("vec", lambda e, par=par: e.tensor_copy(
                    xbfb[:], xld[par][:]), waits=[lm, pe_mile], inc=MVEC)
                for tq in range(2):
                    for s_i, src_sb in ((0, xbfb), (1, sqb)):
                        bk = g * 4 + tq * 2 + s_i
                        pe_mile = p.emit("pe", lambda e, bk=bk, tq=tq, src_sb=src_sb, cc=cc: e.matmul(
                            PS[0:1, bk * 512:(bk + 1) * 512],
                            onescb[:],
                            src_sb[:, tq * 512:(tq + 1) * 512],
                            start=(cc == 0), stop=(cc == NCH - 1)),
                            waits=[sm, vm], inc=MPE)
                buf_guard[par] = pe_mile
        ev = None
        for g in range(NG):
            for tq in range(2):
                for s_i, dst in ((0, r1), (1, r2)):
                    bk = g * 4 + tq * 2 + s_i
                    t0 = g * 1024 + tq * 512
                    ev = p.emit("vec", lambda e, bk=bk, dst=dst, t0=t0: e.tensor_copy(
                        dst[:, t0:t0 + 512], PS[0:1, bk * 512:(bk + 1) * 512]),
                        waits=[pe_mile], inc=MVEC)
        ev = p.emit("vec", lambda e: e.tensor_scalar_mul(r1, r1, 1.0 / D),
                    waits=[ev], hard_waits=[ev], inc=MVEC)
        ev = p.emit("vec", lambda e: e.tensor_scalar_mul(r2, r2, 1.0 / D),
                    waits=[], hard_waits=[ev], inc=MVEC)
        ev = p.emit("vec", lambda e: e.tensor_mul(r3, r1, r1),
                    waits=list(deps), hard_waits=[ev], inc=MVEC)
        ev = p.emit("vec", lambda e: e.tensor_sub(r2, r2, r3),
                    waits=[], hard_waits=[ev], inc=MVEC)
        sm = p.emit("scl", lambda e: e.activation(r3, r2, AF.Sqrt, bias=epscb[0:1, :]),
                    waits=[ev, consts_ready], inc=MSCL)
        ev = p.emit("vec", lambda e: e.reciprocal(r2, r3), waits=[sm], inc=MVEC)
        ev = p.emit("vec", lambda e: e.tensor_mul(r1, r1, r2),
                    waits=[], hard_waits=[ev], inc=MVEC)
        _, s1m = rrw.load(lambda e: e.dma_start(out=RROWS[0], in_=rows1[0:1, :]),
                          waits=[ev])
        _, s2m = rrw.load(lambda e: e.dma_start(out=RROWS[1], in_=rows2[0:1, :]),
                          waits=[ev])
        return [s1m, s2m]

    def build_rmr(g, rows_ready):
        guard = rmr_guard[0]
        ms = []
        for dst_off, row in ((0, 1), (1024, 0)):
            _, m = rbc.load(lambda e, g=g, dst_off=dst_off, row=row: e.indirect_dma_start(
                rmr[:, dst_off:dst_off + 1024], None,
                RROWS,
                bass.IndirectOffsetOnAxis(ap=bidxb[:, row:row + 1], axis=0),
                element_offset=g * 1024),
                waits=list(rows_ready) + [consts_ready, guard])
            ms.append(m)
        return ms

    def ln_chunk(par, lm, dst_ap, rmr_ready, extra=()):
        v1 = p.emit("vec", lambda e, par=par: e.tensor_mul(
            lnscr[:], xld[par][:], rmr[:, 0:1024]),
            waits=[lm] + list(rmr_ready) + list(extra), inc=MVEC)
        v2 = p.emit("vec", lambda e, dst_ap=dst_ap: e.tensor_sub(
            dst_ap, lnscr[:], rmr[:, 1024:2048]), waits=[v1], inc=MVEC)
        return v2

    def ln_phase(src, rows_ready, g, group_guard, to_xns=False, store_guard=None):
        if to_xns and os.environ.get("LN_NORMR", "0") == "1":
            rmr_ready = rows_ready
        else:
            rmr_ready = build_rmr(g, rows_ready)
        last_ln = {}
        miles = []
        stores = []
        ncc_knob = int(os.environ.get("LN_NCC", "32")) if to_xns else NCH
        for cc in range(ncc_knob):
            par, lm = xst.load(
                lambda e, g=g, cc=cc, src=src, b=xst.i % 2: e.dma_start(
                    out=xld[b][:], in_=src[g, cc * 128:(cc + 1) * 128, :]),
                waits=[last_ln.get(xst.i % 2)] + list(rows_ready))
            if to_xns:
                spar = cc % 2
                v = ln_chunk(par, lm, xnstg[spar][:], rmr_ready,
                             extra=[group_guard, store_guard[spar]])
                if os.environ.get("FFN_NOST", "0") != "1":
                    _, smn = stn.load(lambda e, cc=cc, g=g, spar=spar: e.dma_start(
                        out=XNS[cc * 128:(cc + 1) * 128, g * 1024:(g + 1) * 1024],
                        in_=xnstg[spar][:]), waits=[v])
                    store_guard[spar] = smn
                    stores.append(smn)
            else:
                v = ln_chunk(par, lm, xn_sb[:, cc * 1024:(cc + 1) * 1024], rmr_ready,
                             extra=[group_guard])
            last_ln[par] = v
            miles.append(v)
        rmr_guard[0] = miles[-1] if miles else rmr_guard[0]
        return miles, stores

    def proj_pass_a(src, rows_ready):
        wo_free = wfree
        qst_free = [None, None]
        vt_free = [None, None]
        evm = {}
        prev_rope = [None, None]
        group_guard = None
        for g in range(NG):
            xn_mile, _ = ln_phase(src, rows_ready, g, group_guard)
            for hf in range(2):
                for tb in range(2):
                    _, cs_ready = mld.load(lambda e, g=g, hf=hf, tb=tb: e.dma_start(
                        out=csb[:, (hf * 2 + tb) * 512:(hf * 2 + tb) * 512 + 512],
                        in_=CSQ[tb, :, g * 1024 + hf * 512:g * 1024 + hf * 512 + 512]),
                        waits=[evm.get("lastk")])
            csk_loaded = False
            mm_done = None
            for pr in range(24):
                kind = "q" if pr < 16 else ("k" if pr < 20 else "v")
                pi = (pr % 2) if kind != "v" else 0
                bank0 = pi * 4
                if kind == "k" and not csk_loaded:
                    for hf in range(2):
                        for tb in range(2):
                            _, cs_ready = mld.load(lambda e, g=g, hf=hf, tb=tb: e.dma_start(
                                out=csb[:, (hf * 2 + tb) * 512:(hf * 2 + tb) * 512 + 512],
                                in_=CSK[tb, :, g * 1024 + hf * 512:g * 1024 + hf * 512 + 512]),
                                waits=[evm.get("lastq")])
                    csk_loaded = True
                if kind == "v":
                    v_guard = [evm.get((1, j, hf)) for j in range(2) for hf in range(2)]
                else:
                    v_guard = []
                for q8 in range(4):
                    wb, wm = wst.load(
                        lambda e, pr=pr, q8=q8, b=wst.i % 3: e.dma_start(
                            out=WG[b][:], in_=WQKV_A[pr, :, q8 * 8:(q8 + 1) * 8, :]),
                        waits=[wo_free.get(wst.i % 3)])
                    for c8 in range(8):
                        cc = q8 * 8 + c8
                        for j in range(2):
                            for hf in range(2):
                                bk = bank0 + j * 2 + hf
                                mm_done = p.emit("pe", lambda e, wb=wb, c8=c8, j=j, hf=hf, bk=bk, cc=cc: e.matmul(
                                    bank(bk),
                                    WG[wb][:, (c8 * 2 + j) * 128:(c8 * 2 + j) * 128 + 128],
                                    xn_sb[:, cc * 1024 + hf * 512:cc * 1024 + hf * 512 + 512],
                                    start=(cc == 0), stop=(cc == NCH - 1)),
                                    waits=[wm, xn_mile[cc], evm.get((pi, j, hf))], inc=MPE)
                    wo_free[wb] = mm_done
                for j in range(2):
                    ch = pr * 2 + j
                    for hf in range(2):
                        bk = bank0 + j * 2 + hf
                        if kind in ("q", "k"):
                            if kind == "q":
                                sl = (ch % 2) * 1024 + hf * 512
                                dst = qst[:, sl:sl + 512]
                                extra = [qst_free[ch % 2]]
                            else:
                                kv = ch - 32
                                base = kv * T + g * 1024 + hf * 512
                                dst = KT[:, base:base + 512]
                                extra = []
                            vc = p.emit("vec", lambda e, bk=bk: e.tensor_copy(
                                qtmp[:], bank(bk)), waits=[mm_done, prev_rope[0]] + extra, inc=MVEC)
                            _, md1 = qsw.load(lambda e: e.dma_start(
                                out=qswp[0:64, :], in_=qtmp[64:128, :]), waits=[vc])
                            _, md2 = qsw.load(lambda e: e.dma_start(
                                out=qswp[64:128, :], in_=qtmp[0:64, :]), waits=[vc])
                            m1 = p.emit("vec", lambda e, hf=hf: e.tensor_mul(
                                qtmp[:], qtmp[:], csb[:, (hf * 2 + 0) * 512:(hf * 2 + 0) * 512 + 512]),
                                waits=[md1, md2, cs_ready], inc=MVEC)
                            m2 = p.emit("vec", lambda e, hf=hf: e.tensor_mul(
                                qswp[:], qswp[:], csb[:, (hf * 2 + 1) * 512:(hf * 2 + 1) * 512 + 512]),
                                waits=[m1], inc=MVEC)
                            v1 = p.emit("vec", lambda e, dst=dst: e.tensor_add(
                                dst, qtmp[:], qswp[:]), waits=[m2], inc=MVEC)
                            prev_rope[0] = v1
                            evm[(pi, j, hf)] = vc
                            if kind == "q":
                                evm["lastq"] = v1
                                if hf == 1:
                                    _, smq = stq.load(lambda e, ch=ch, g=g: e.dma_start(
                                        out=QTS[g, ch], in_=qst[:, (ch % 2) * 1024:(ch % 2) * 1024 + 1024]),
                                        waits=[prev_rope[1], v1])
                                    qst_free[ch % 2] = smq
                                    prev_rope[1] = None
                                else:
                                    prev_rope[1] = v1
                            else:
                                evm["lastk"] = v1
                        else:
                            kv = ch - 40
                            vp = hf
                            sc_copy = p.emit("scl", lambda e, bk=bk, vp=vp: e.copy(
                                vtmp[:, vp * 512:vp * 512 + 512], bank(bk)),
                                waits=[mm_done, vt_free[vp]], inc=MSCL)
                            trs = None
                            for k in range(4):
                                trs = p.emit("pe", lambda e, k=k, vp=vp: e.transpose(
                                    bankbf(4, k * 128, 128),
                                    vtmp[:, vp * 512 + k * 128:vp * 512 + (k + 1) * 128],
                                    idenb[:]),
                                    waits=[sc_copy, evm.get("vtev")] + v_guard, inc=MPE)
                            sc0 = g * 8 + hf * 4
                            evv = p.emit("vec", lambda e, sc0=sc0, kv=kv: e.tensor_copy(
                                VT[:, :].rearrange("p (sc u d) -> p sc u d", sc=16, u=8)[:, sc0:sc0 + 4, kv, :],
                                bankbf(4, 0, 512).rearrange("p (k d) -> p k d", k=4)),
                                waits=[trs], inc=MVEC)
                            evm["vtev"] = evv
                            vt_free[vp] = evv
                            evm[(0, j, hf)] = evv
            group_guard = mm_done
        return qst_free

    def proj_pass_b(src, rows_ready):
        wo_free = wfree
        qst_free = [None, None]
        vt_free = [None, None]
        evm = {}
        bkg = {}
        group_guard = None
        for g in range(NG):
            xn_mile, _ = ln_phase(src, rows_ready, g, group_guard)
            mm_done = None
            for u in range(8):
                pi = u % 2
                for gr in range(8):
                    wb, wm = wst.load(
                        lambda e, u=u, gr=gr, b=wst.i % 3: e.dma_start(
                            out=WG[b][:], in_=WQ_B[u, :, gr * 4:(gr + 1) * 4, :, :]),
                        waits=[wo_free.get(wst.i % 3)])
                    for c4 in range(4):
                        cc = gr * 4 + c4
                        for gg in range(4):
                            for hf in range(2):
                                bk = pi * 2 + hf
                                mm_done = p.emit("pe", lambda e, wb=wb, c4=c4, gg=gg, hf=hf, bk=bk, cc=cc: e.matmul(
                                    bank(bk),
                                    WG[wb][:, (c4 * 4 + gg) * 128:(c4 * 4 + gg) * 128 + 128],
                                    xn_sb[:, cc * 1024 + hf * 512:cc * 1024 + hf * 512 + 512],
                                    start=(cc == 0 and gg == 0), stop=(cc == NCH - 1 and gg == 3)),
                                    waits=[wm, xn_mile[cc], bkg.get(bk)], inc=MPE)
                    wo_free[wb] = mm_done
                for hf in range(2):
                    bk = pi * 2 + hf
                    sc = p.emit("scl", lambda e, bk=bk, u=u, hf=hf: e.activation(
                        qst[:, (u % 2) * 1024 + hf * 512:(u % 2) * 1024 + hf * 512 + 512],
                        bank(bk), AF.Copy, scale=ISQ),
                        waits=[mm_done, qst_free[u % 2]], inc=MSCL)
                    bkg[bk] = sc
                _, smq = stq.load(lambda e, u=u, g=g: e.dma_start(
                    out=QTS[g, u], in_=qst[:, (u % 2) * 1024:(u % 2) * 1024 + 1024]),
                    waits=[bkg[pi * 2], bkg[pi * 2 + 1]])
                qst_free[u % 2] = smq

            for pr in range(8):
                kind = "k" if pr < 4 else "v"
                pi = pr % 2
                bank0 = (pi * 4) if kind == "k" else 4
                v_guard = [bkg.get(0)] if kind == "v" else []
                mm_done = None
                for q8 in range(4):
                    wb, wm = wst.load(
                        lambda e, pr=pr, q8=q8, b=wst.i % 3: e.dma_start(
                            out=WG[b][:], in_=WKV_B[pr, :, q8 * 8:(q8 + 1) * 8, :]),
                        waits=[wo_free.get(wst.i % 3)])
                    for c8 in range(8):
                        cc = q8 * 8 + c8
                        for j in range(2):
                            for hf in range(2):
                                bk = bank0 + j * 2 + hf
                                mm_done = p.emit("pe", lambda e, wb=wb, c8=c8, j=j, hf=hf, bk=bk, cc=cc: e.matmul(
                                    bank(bk),
                                    WG[wb][:, (c8 * 2 + j) * 128:(c8 * 2 + j) * 128 + 128],
                                    xn_sb[:, cc * 1024 + hf * 512:cc * 1024 + hf * 512 + 512],
                                    start=(cc == 0), stop=(cc == NCH - 1)),
                                    waits=[wm, xn_mile[cc], bkg.get(bk)], inc=MPE)
                    wo_free[wb] = mm_done
                for j in range(2):
                    ch = pr * 2 + j
                    for hf in range(2):
                        bk = bank0 + j * 2 + hf
                        if kind == "k":
                            kv = ch
                            base = kv * T + g * 1024 + hf * 512
                            evk = p.emit("scl", lambda e, bk=bk, base=base: e.copy(
                                KT[:, base:base + 512], bank(bk)),
                                waits=[mm_done], inc=MSCL)
                            bkg[bk] = evk
                        else:
                            kv = ch - 8
                            vp = hf
                            sc_copy = p.emit("scl", lambda e, bk=bk, vp=vp: e.copy(
                                vtmp[:, vp * 512:vp * 512 + 512], bank(bk)),
                                waits=[mm_done, vt_free[vp]], inc=MSCL)
                            bkg[bk] = sc_copy
                            trs = None
                            for k in range(4):
                                trs = p.emit("pe", lambda e, k=k, vp=vp: e.transpose(
                                    bankbf(0, k * 128, 128),
                                    vtmp[:, vp * 512 + k * 128:vp * 512 + (k + 1) * 128],
                                    idenb[:]),
                                    waits=[sc_copy, evm.get("vtev")] + v_guard, inc=MPE)
                            sc0 = g * 8 + hf * 4
                            evv = p.emit("vec", lambda e, sc0=sc0, kv=kv: e.tensor_copy(
                                VT[:, :].rearrange("p (sc u d) -> p sc u d", sc=16, u=8)[:, sc0:sc0 + 4, kv, :],
                                bankbf(0, 0, 512).rearrange("p (k d) -> p k d", k=4)),
                                waits=[trs], inc=MVEC)
                            evm["vtev"] = evv
                            vt_free[vp] = evv
                            bkg[0] = evv
            group_guard = mm_done
        return qst_free

    def attention_block(blk, variant, q_ready, oblk_free, ptb_free, dbg=None):
        heads = 32 if variant == "a" else 8
        qld_free = qldfree
        last_o_ev = None
        exp_prev = oblk_free
        trs_prev = None
        pt_ev_prev = None
        pv_last = ptb_free
        for hi in range(heads):
            kv = hi // 4 if variant == "a" else hi
            h = hi
            qb, qm = qsx.load(lambda e, h=h, blk=blk, b=qsx.i % 2: e.dma_start(
                out=qld[:, b * 1024:b * 1024 + 1024], in_=QTS[blk, h]),
                waits=list(q_ready) + [qld_free.get(qsx.i % 2)])
            for sub in range(2):
                qbase = blk * 1024 + sub * 512
                nsc_full = 16 if variant == "b" else (qbase + 512) // 128
                first_ev_guard = pv_last
                if variant == "a":
                    dsc0 = qbase // 128
                    pt_ev_prev = p.emit("vec", lambda e, dsc0=dsc0: e.memset(
                        ptb.rearrange("p (sc q) -> p sc q", sc=16)[:, dsc0:dsc0 + 4, :], 0.0),
                        waits=[pv_last, pt_ev_prev], inc=MVEC)
                    first_ev_guard = None
                for qt4 in range(4):
                    qt = sub * 4 + qt4
                    qabs = blk * 8 + qt
                    slen = (qabs + 1) * 128 if variant == "a" else T
                    nS = (slen + 511) // 512
                    par = qt % 2
                    sm_last = None
                    for sci in range(nS):
                        w = min(512, slen - sci * 512)
                        sm_last = p.emit("pe", lambda e, qb=qb, qt=qt, kv=kv, sci=sci, w=w: e.matmul(
                            PS[:, sci * 512:sci * 512 + w],
                            qld[:, qb * 1024 + qt * 128:qb * 1024 + qt * 128 + 128],
                            KT[:, kv * T + sci * 512:kv * T + sci * 512 + w],
                            start=True, stop=True),
                            waits=[qm, exp_prev], inc=MPE)
                    if variant == "a":
                        vmk = p.emit("vec", lambda e, slen=slen: e.tensor_add(
                            PS[:, slen - 128:slen], PS[:, slen - 128:slen], maskb[:]),
                            waits=[sm_last], inc=MVEC)
                    else:
                        vmk = sm_last
                    vmx = p.emit("vec", lambda e, slen=slen, par=par: e.reduce_max(
                        colsb[:, par * 4:par * 4 + 1], PS[:, 0:slen], AX.X, negate=True),
                        waits=[vmk], inc=MVEC)
                    exp_m = p.emit("scl", lambda e, slen=slen, par=par: e.activation(
                        pbuf[:, 0:slen], PS[:, 0:slen], AF.Exp,
                        bias=colsb[:, par * 4:par * 4 + 1],
                        accum_out=colsb[:, par * 4 + 1:par * 4 + 2]),
                        waits=[vmx, trs_prev], inc=MSCL)
                    exp_prev = exp_m
                    vrc = p.emit("vec", lambda e, par=par: e.reciprocal(
                        colsb[:, par * 4 + 2:par * 4 + 3], colsb[:, par * 4 + 1:par * 4 + 2]),
                        waits=[exp_m], inc=MVEC)
                    vnm = p.emit("vec", lambda e, slen=slen, par=par: e.tensor_scalar_mul(
                        pbuf[:, 0:slen], pbuf[:, 0:slen], colsb[:, par * 4 + 2:par * 4 + 3]),
                        waits=[], hard_waits=[vrc], inc=MVEC)
                    n128 = slen // 128
                    trs = None
                    for sc in range(n128):
                        trs = p.emit("pe", lambda e, sc=sc: e.transpose(
                            PSbf[:, 4 * 1024 + sc * 128:4 * 1024 + (sc + 1) * 128],
                            pbuf[:, sc * 128:(sc + 1) * 128], idenb[:]),
                            waits=[vnm, pt_ev_prev], inc=MPE)
                    trs_prev = trs
                    pt_ev_prev = p.emit("vec", lambda e, n128=n128, qt4=qt4: e.tensor_copy(
                        ptb.rearrange("p (sc q) -> p sc q", sc=16)[:, 0:n128, qt4 * 128:(qt4 + 1) * 128],
                        PSbf[:, 4 * 1024:4 * 1024 + n128 * 128].rearrange("p (sc d) -> p sc d", d=128)),
                        waits=[trs, first_ev_guard], inc=MVEC)
                    first_ev_guard = None
                    if dbg is not None and hi == 0 and blk == 0 and sub == 0 and qt4 == 3:
                        d1 = p.emit("sync", lambda e: e.dma_start(
                            out=dbg["pb"], in_=pbuf[:]), waits=[vnm], inc="dbg", amt=16)
                        d2 = p.emit("sync", lambda e: e.dma_start(
                            out=dbg["cols"], in_=colsb[:]), waits=[vnm], inc="dbg", amt=16)
                        d3 = p.emit("sync", lambda e: e.dma_start(
                            out=dbg["ptb"], in_=ptb[:]), waits=[pt_ev_prev], inc="dbg", amt=16)
                        bl = Mile("dbg", p.cnt["dbg"], None)
                        dbg["st"] += [d1, d2, d3]
                        exp_prev = bl
                        pt_ev_prev = bl
                        trs_prev = bl
                pv_last = None
                for sc in range(nsc_full):
                    pv_last = p.emit("pe", lambda e, sc=sc, kv=kv, first=(sc == 0), last=(sc == nsc_full - 1): e.matmul(
                        bank(6),
                        VT[:, (sc * 8 + kv) * 128:(sc * 8 + kv) * 128 + 128],
                        ptb[:, sc * 512:(sc + 1) * 512],
                        start=first, stop=last),
                        waits=[pt_ev_prev, last_o_ev], inc=MPE)
                last_o_ev = p.emit("vec", lambda e, h=h, sub=sub: e.tensor_copy(
                    oblk[:, h * 1024 + sub * 512:h * 1024 + sub * 512 + 512], bank(6)),
                    waits=[pv_last, oblk_free], inc=MVEC)
            qld_free[qb] = last_o_ev
        return last_o_ev, pv_last

    def wo_epilogue(blk, variant, o_ready, src, dst):
        ncc = 32 if variant == "a" else 8
        wsrc = WO_A if variant == "a" else WO_B
        ngr = 4 if variant == "a" else 1
        wo_free = wfree
        epi_free = {}
        xold_free = [None, None]
        xnst_free = None
        last_store = None
        for pr in range(16):
            opar, om = oldst.load(lambda e, pr=pr, blk=blk, src=src, b=oldst.i % 2: e.dma_start(
                out=xoldb[b][:].rearrange("p (c t) -> p c t", c=2),
                in_=src[blk, pr * 256:(pr + 1) * 256, :].rearrange("(c p) t -> p c t", p=128)),
                waits=[xold_free[oldst.i % 2], o_ready])
            mm_done = None
            for gri in range(ngr):
                wb, wm = wst.load(
                    lambda e, pr=pr, gri=gri, wsrc=wsrc, b=wst.i % 3: e.dma_start(
                        out=WG[b][:], in_=wsrc[pr, :, gri * 8:(gri + 1) * 8, :]),
                    waits=[wo_free.get(wst.i % 3)])
                for c8 in range(8):
                    cc = gri * 8 + c8
                    if cc >= ncc:
                        break
                    for j in range(2):
                        for hf in range(2):
                            bk = j * 2 + hf
                            mm_done = p.emit("pe", lambda e, wb=wb, c8=c8, j=j, hf=hf, bk=bk, cc=cc: e.matmul(
                                bank(bk),
                                WG[wb][:, (c8 * 2 + j) * 128:(c8 * 2 + j) * 128 + 128],
                                oblk[:, cc * 1024 + hf * 512:cc * 1024 + hf * 512 + 512],
                                start=(cc == 0), stop=(cc == ncc - 1)),
                                waits=[wm, o_ready, epi_free.get((j, hf))], inc=MPE)
                wo_free[wb] = mm_done
            adds = []
            for j in range(2):
                for hf in range(2):
                    bk = j * 2 + hf
                    va = p.emit("vec", lambda e, bk=bk, j=j, hf=hf, opar=opar: e.tensor_add(
                        xnstb[:, j * 1024 + hf * 512:j * 1024 + hf * 512 + 512],
                        bank(bk),
                        xoldb[opar][:, j * 1024 + hf * 512:j * 1024 + hf * 512 + 512]),
                        waits=[mm_done, om, xnst_free], inc=MVEC)
                    epi_free[(j, hf)] = va
                    adds.append(va)
            xold_free[opar] = adds[-1]
            _, smx = stx.load(lambda e, pr=pr, blk=blk, dst=dst: e.dma_start(
                out=dst[blk, pr * 256:(pr + 1) * 256, :].rearrange("(c p) t -> p c t", p=128),
                in_=xnstb[:].rearrange("p (c t) -> p c t", c=2)),
                waits=adds)
            xnst_free = smx
            last_store = smx
        return last_store

    def ffn_pass(src, dst, rows_ready, prev_guard):
        store_guard = [None, None]
        group_guard = None
        if os.environ.get("FFN_PH", "2") != "0":
            for g in range(NG):
                _, _ = ln_phase(src, rows_ready, g, group_guard, to_xns=True,
                                store_guard=store_guard)
        xn_done = stn.final_miles()

        wo_free = wfree
        xg_free = [None, None]
        xres_free = None
        up_free = {}
        dn_free = {}
        ug_free = {}
        prev_scat = []
        nh = int(os.environ.get("FFN_NH", "16"))
        for h in range(nh):
            gp = h % 2
            gms = []
            for k in range(2):
                _, gm = gxg.load(lambda e, h=h, k=k, gp=gp: e.indirect_dma_start(
                    xg_sb[gp][:, k * 2048:(k + 1) * 2048], None,
                    XNS, bass.IndirectOffsetOnAxis(ap=permb[:, h * 2 + k:h * 2 + k + 1], axis=0)),
                    waits=xn_done + [xg_free[gp], consts_ready] + list(rows_ready))
                gms.append(gm)
            rms = []
            for k in range(2):
                for g in range(NG):
                    _, rm = grs.load(lambda e, h=h, k=k, g=g, src=src: e.indirect_dma_start(
                        xresb[:, (k * 2 + g) * 1024:(k * 2 + g + 1) * 1024], None,
                        src.rearrange("g d t -> (g d) t"),
                        bass.IndirectOffsetOnAxis(ap=permb[:, h * 2 + k:h * 2 + k + 1], axis=0),
                        element_offset=g * D * TG),
                        waits=[xres_free, consts_ready] + list(rows_ready) + list(prev_guard) + xn_done)
                    rms.append(rm)
            wbu, wmu = wst.load(lambda e, h=h, b=wst.i % 3: e.dma_start(
                out=WG[b][:], in_=WUP[h]), waits=[wo_free.get(wst.i % 3)])
            mmu = {}
            mm_last_up = None
            for tq in range(4):
                for m4 in range(2):
                    pi = (tq * 2 + m4) % 2
                    last = None
                    for mi in range(4):
                        mc = m4 * 4 + mi
                        for dc in range(2):
                            last = p.emit("pe", lambda e, wbu=wbu, mc=mc, dc=dc, tq=tq, pi=pi, mi=mi, gp=gp: e.matmul(
                                bank(pi * 4 + mi),
                                WG[wbu][:, dc * 1024 + mc * 128:dc * 1024 + mc * 128 + 128],
                                xg_sb[gp][:, dc * 2048 + tq * 512:dc * 2048 + tq * 512 + 512],
                                start=(dc == 0), stop=(dc == 1)),
                                waits=[wmu] + gms + [up_free.get((pi, mi))], inc=MPE)
                    mm_last_up = last
                    for mi in range(4):
                        mc = m4 * 4 + mi
                        gv = p.emit("scl", lambda e, pi=pi, mi=mi, mc=mc, tq=tq: e.activation(
                            ug_sb[:, mc * 2048 + tq * 512:mc * 2048 + tq * 512 + 512],
                            bank(pi * 4 + mi), AF.Gelu),
                            waits=[last, ug_free.get((mc, tq))], inc=MSCL)
                        up_free[(pi, mi)] = gv
                        mmu[(mc, tq)] = gv
            wo_free[wbu] = mm_last_up
            wbd, wmd = wst.load(lambda e, h=h, b=wst.i % 3: e.dma_start(
                out=WG[b][:], in_=WDN[h]), waits=[wo_free.get(wst.i % 3)])
            gel_all = p.now(MSCL, "scl")
            dn_free[0] = gel_all
            dn_free[1] = gel_all
            h_adds = []
            for tq in range(4):
                g, hf = tq // 2, tq % 2
                for j2 in range(2):
                    bk = j2
                    last = None
                    for mc in range(8):
                        last = p.emit("pe", lambda e, wbd=wbd, mc=mc, j2=j2, tq=tq, bk=bk: e.matmul(
                            bank(bk),
                            WG[wbd][:, mc * 256 + j2 * 128:mc * 256 + j2 * 128 + 128],
                            ug_sb[:, mc * 2048 + tq * 512:mc * 2048 + tq * 512 + 512],
                            start=(mc == 0), stop=(mc == 7)),
                            waits=[wmd, mmu[(mc, tq)], dn_free.get(bk)], inc=MPE)
                    va = p.emit("vec", lambda e, bk=bk, j2=j2, g=g, hf=hf: e.tensor_add(
                        xnew_ffn[:, (j2 * 2 + g) * 1024 + hf * 512:(j2 * 2 + g) * 1024 + hf * 512 + 512],
                        bank(bk),
                        xresb[:, (j2 * 2 + g) * 1024 + hf * 512:(j2 * 2 + g) * 1024 + hf * 512 + 512]),
                        waits=[last] + rms + prev_scat, inc=MVEC)
                    dn_free[bk] = va
                    h_adds.append(va)
            wo_free[wbd] = h_adds[-1]
            prev_scat = []
            for k in range(2):
                for g in range(NG):
                    _, scm = stsc.load(lambda e, h=h, k=k, g=g, dst=dst: e.indirect_dma_start(
                        dst.rearrange("g d t -> (g d) t"),
                        bass.IndirectOffsetOnAxis(ap=permb[:, h * 2 + k:h * 2 + k + 1], axis=0),
                        xnew_ffn[:, (k * 2 + g) * 1024:(k * 2 + g + 1) * 1024], None,
                        element_offset=g * D * TG),
                        waits=h_adds)
                    prev_scat.append(scm)
            prev_scat = prev_scat[-2:]
            xg_free[gp] = h_adds[-1]
            xres_free = h_adds[-1]
            for mi in range(4):
                up_free[(0, mi)] = h_adds[-1]
            for tq in range(4):
                for mc in range(8):
                    ug_free[(mc, tq)] = h_adds[-1]
        return stsc.final_miles()

    seq = [("a", XIN, XA), ("f", XA, XB), ("b", XB, XA), ("f", XA, XOUT)]
    seq = seq[:n_sub]
    kind0, s0, _ = seq[-1]
    seq[-1] = (kind0, s0, XOUT)

    prev_done = ()
    final_stream = None
    first = True
    for kind, src, dst in seq:
        rows_ready = stats_pass(src, deps=prev_done)
        if debug and first:
            dbg_st.append(p.emit("sync", lambda e: e.dma_start(out=D_ROWS[0], in_=rows1[:]),
                                 waits=list(rows_ready), inc="dbg", amt=16))
            dbg_st.append(p.emit("sync", lambda e: e.dma_start(out=D_ROWS[1], in_=rows2[:]),
                                 waits=list(rows_ready), inc="dbg", amt=16))
        if kind in ("a", "b"):
            if kind == "a":
                qfree = proj_pass_a(src, rows_ready)
            else:
                qfree = proj_pass_b(src, rows_ready)
            q_ready = [m for m in qfree if m is not None]
            vec_now = p.now(MVEC, "vec")
            scl_now = p.now(MSCL, "scl")
            pe_now = p.now(MPE, "pe")
            p.barrier_wait("pe", [vec_now, scl_now])
            if debug and first:
                dbg_st.append(p.emit("sync", lambda e: e.dma_start(out=D_KT, in_=KT[:]),
                                     waits=[vec_now, scl_now], inc="dbg", amt=16))
                dbg_st.append(p.emit("sync", lambda e: e.dma_start(out=D_VT, in_=VT[:]),
                                     waits=[vec_now, scl_now], inc="dbg", amt=16))
                for gdb in range(NG):
                    for hdb in range(32):
                        dbg_st.append(p.emit("sync", lambda e, gdb=gdb, hdb=hdb: e.dma_start(
                            out=D_QTS[gdb, hdb], in_=QTS[gdb, hdb]),
                            waits=[vec_now] + q_ready, inc="dbg", amt=16))
            stores = []
            oblk_free = vec_now
            ptb_free = None
            o_ready = None
            for blk in range(2):
                o_ready, pv_last = attention_block(blk, kind, q_ready, oblk_free, ptb_free,
                                                   dbg=dbg_att if (debug and first) else None)
                if debug and first and blk == 0:
                    dbg_st.append(p.emit("sync", lambda e: e.dma_start(out=D_OBLK, in_=oblk[:, 0:32768]),
                                         waits=[o_ready], inc="dbg", amt=16))
                stv = wo_epilogue(blk, kind, o_ready, src, dst)
                stores.append(stv)
                oblk_free = stv
                ptb_free = stv
            prev_done = tuple(stores) + tuple(stx.final_miles())
            final_stream = stx
        else:
            sc = ffn_pass(src, dst, rows_ready, prev_done)
            prev_done = tuple(sc)
            final_stream = stsc

        first = False
    final_miles = list(prev_done) + final_stream.final_miles()
    if debug:
        final_miles += [Mile("dbg", p.cnt.get("dbg", 0), None)]
    p.barrier_wait("sync", final_miles)
    p.barrier_wait("gps", final_miles)

    p.check_deadlock()
    block = ctx.enter_context(nc.Block())
    p.replay_into(block)
    ctx.close()
    return nc


def _prelayout(inputs):
    f32 = np.float32
    Wq = np.asarray(inputs["L0_attn_a_Wq"], f32)
    Wk = np.asarray(inputs["L0_attn_a_Wk"], f32)
    Wv = np.asarray(inputs["L0_attn_a_Wv"], f32)
    Wo = np.asarray(inputs["L0_attn_a_Wo"], f32)
    wq = np.asarray(inputs["L0_attn_b_wq"], f32)
    wk = np.asarray(inputs["L0_attn_b_wk"], f32)
    wv = np.asarray(inputs["L0_attn_b_wv"], f32)
    wo = np.asarray(inputs["L0_attn_b_wo"], f32)
    up = np.asarray(inputs["L0_ffn_up"], f32)
    dn = np.asarray(inputs["L0_ffn_down"], f32)
    perm = np.asarray(inputs["L0_ffn_perm"]).astype(np.uint32)

    d = {}
    wcat = np.concatenate([Wq, Wk, Wv], axis=1)
    d["wqkv_a"] = np.ascontiguousarray(wcat.reshape(NCH, 128, 24, 256).transpose(2, 1, 0, 3))
    d["wo_a"] = np.ascontiguousarray(Wo.reshape(NCH, 128, 16, 256).transpose(2, 1, 0, 3))
    d["wq_b"] = np.ascontiguousarray(wq.reshape(NCH, 128, 8, 4, 128).transpose(2, 1, 0, 3, 4))
    wkvcat = np.concatenate([wk, wv], axis=1)
    d["wkv_b"] = np.ascontiguousarray(wkvcat.reshape(NCH, 128, 8, 256).transpose(2, 1, 0, 3))
    d["wo_b"] = np.ascontiguousarray(wo.reshape(8, 128, 16, 256).transpose(2, 1, 0, 3))
    d["wup"] = np.ascontiguousarray(up.reshape(16, 2, 128, 1024).transpose(0, 2, 1, 3))
    d["wdn"] = np.ascontiguousarray(dn.reshape(16, 8, 128, 256).transpose(0, 2, 1, 3))
    d["permc"] = np.ascontiguousarray(perm.reshape(32, 128).T)

    half = 64
    invf = 1.0 / 10000.0 ** (np.arange(half, dtype=np.float32) / np.float32(half))
    pos = np.arange(T, dtype=np.float32)
    ang = invf[:, None] * pos[None, :]
    cos = np.cos(ang).astype(f32)
    sin = np.sin(ang).astype(f32)
    cs1 = np.concatenate([cos, cos], axis=0)
    cs2 = np.concatenate([-sin, sin], axis=0)
    d["csq"] = (np.stack([cs1, cs2]) * np.float32(ISQ)).astype(f32)
    d["csk"] = np.stack([cs1, cs2]).astype(f32)

    ii, jj = np.indices((128, 128))
    d["maskc"] = np.where(jj <= ii, 0.0, -1e30).astype(f32)
    d["idenb"] = np.eye(128).astype(ml_dtypes.bfloat16)
    d["onescb"] = np.ones((128, 1), dtype=ml_dtypes.bfloat16)
    d["onesr"] = np.ones((1, 128), dtype=f32)
    d["epsc"] = np.full((128, 1), 1e-5, dtype=f32)
    d["bidx"] = np.tile(np.array([[0, 1]], dtype=np.uint32), (128, 1))
    return d


def kernel(L0_attn_a_X, L0_attn_a_Wq, L0_attn_a_Wk, L0_attn_a_Wv, L0_attn_a_Wo,
           L0_attn_b_wq, L0_attn_b_wk, L0_attn_b_wv, L0_attn_b_wo,
           L0_ffn_up, L0_ffn_down, L0_ffn_perm, n_sub=4, debug=False):
    inputs = dict(
        L0_attn_a_Wq=L0_attn_a_Wq, L0_attn_a_Wk=L0_attn_a_Wk,
        L0_attn_a_Wv=L0_attn_a_Wv, L0_attn_a_Wo=L0_attn_a_Wo,
        L0_attn_b_wq=L0_attn_b_wq, L0_attn_b_wk=L0_attn_b_wk,
        L0_attn_b_wv=L0_attn_b_wv, L0_attn_b_wo=L0_attn_b_wo,
        L0_ffn_up=L0_ffn_up, L0_ffn_down=L0_ffn_down, L0_ffn_perm=L0_ffn_perm,
    )
    shared = _prelayout(inputs)
    X = np.asarray(L0_attn_a_X, np.float32)
    nc = build_program(n_sub=n_sub, debug=debug)
    in_maps = []
    for c in range(8):
        xt = np.ascontiguousarray(X[c].T.reshape(D, NG, TG).transpose(1, 0, 2))
        m = dict(shared)
        m["xin"] = xt
        in_maps.append(m)
    res = run_bass_kernel_spmd(nc, in_maps, core_ids=list(range(8)))
    outs = []
    for c in range(8):
        xo = np.asarray(res.results[c]["xout"])
        outs.append(xo.transpose(1, 0, 2).reshape(D, T).T)
    out = np.ascontiguousarray(np.stack(outs).astype(np.float32))
    if debug:
        return out, res.results
    return out


# version 02
# speedup vs baseline: 1.1106x; 1.1106x over previous
import os
import numpy as np
import ml_dtypes
from contextlib import ExitStack

import concourse.bass as bass
import concourse.mybir as mybir
from concourse.bass_utils import run_bass_kernel_spmd

F32 = mybir.dt.float32
BF16 = mybir.dt.bfloat16
U32 = mybir.dt.uint32
U8 = mybir.dt.uint8
AF = mybir.ActivationFunctionType
ALU = mybir.AluOpType
AX = mybir.AxisListType

T = 2048
D = 4096
NCH = 32
NG = 2
TG = 1024
ISQ = float(1.0 / np.sqrt(128.0))
EPS = 1e-5

ENGS = ("pe", "vec", "scl", "gps", "sync")


class Mile:
    __slots__ = ("sem", "val", "eng")

    def __init__(self, sem, val, eng):
        self.sem, self.val, self.eng = sem, val, eng


class Plan:
    def __init__(self, nc, ctx):
        self.nc = nc
        self.ctx = ctx
        self.prog = {e: [] for e in ENGS}
        self.semh = {}
        self.cnt = {}

    def sem(self, name):
        if name not in self.semh:
            self.semh[name] = self.ctx.enter_context(self.nc.semaphore(name))
            self.cnt[name] = 0
        return name

    def emit(self, eng, fn, waits=(), inc=None, amt=1, hard_waits=()):
        ws = []
        for m in waits:
            if m is None or m.eng == eng:
                continue
            ws.append((m.sem, m.val))
        for m in hard_waits:
            if m is None:
                continue
            ws.append((m.sem, m.val))
        if inc is not None:
            self.sem(inc)
            self.cnt[inc] += amt
            rec = (inc, self.cnt[inc], amt)
            mile = Mile(inc, self.cnt[inc], eng if amt == 1 else None)
        else:
            rec, mile = None, None
        self.prog[eng].append((fn, tuple(ws), rec))
        return mile

    def now(self, semname, eng=None):
        return Mile(semname, self.cnt.get(semname, 0), eng)

    def barrier_wait(self, eng, miles):
        ws = [m for m in miles if m is not None and m.eng != eng]
        if not ws:
            return
        self.prog[eng].append((lambda e: e.nop(), tuple((m.sem, m.val) for m in ws), None))

    def check_deadlock(self):
        heads = {e: 0 for e in ENGS}
        counts = {}
        progressed = True
        while progressed:
            progressed = False
            for e in ENGS:
                while heads[e] < len(self.prog[e]):
                    fn, ws, rec = self.prog[e][heads[e]]
                    if all(counts.get(sm, 0) >= v for sm, v in ws):
                        if rec is not None:
                            sm, v, a = rec
                            counts[sm] = counts.get(sm, 0) + a
                        heads[e] += 1
                        progressed = True
                    else:
                        break
        stuck = {e: heads[e] for e in ENGS if heads[e] < len(self.prog[e])}
        if stuck:
            msg = []
            for e, h in stuck.items():
                fn, ws, rec = self.prog[e][h]
                missing = [(sm, v, counts.get(sm, 0)) for sm, v in ws
                           if counts.get(sm, 0) < v]
                msg.append(f"{e}[{h}/{len(self.prog[e])}] blocked on {missing[:4]}")
            raise RuntimeError("DEADLOCK:\n" + "\n".join(msg))

    def replay_into(self, block):
        def mk(eng):
            def run(e):
                seen = {}
                for fn, ws, rec in self.prog[eng]:
                    for s, v in ws:
                        if seen.get(s, -1) < v:
                            e.wait_ge(self.semh[s], v)
                            seen[s] = v
                    inst = fn(e)
                    if rec is not None:
                        s, v, a = rec
                        inst.then_inc(self.semh[s], a)
            return run

        block.tensor(mk("pe"))
        block.vector(mk("vec"))
        block.scalar(mk("scl"))
        block.gpsimd(mk("gps"))
        block.sync(mk("sync"))


class Stream:
    def __init__(self, plan, name, eng, nbuf):
        self.plan = plan
        self.eng = eng
        self.nbuf = nbuf
        self.sems = [plan.sem(f"{name}{i}") for i in range(nbuf)]
        self.i = 0

    def load(self, fn, waits=()):
        b = self.i % self.nbuf
        m = self.plan.emit(self.eng, fn, waits=waits, inc=self.sems[b], amt=16)
        self.i += 1
        return b, Mile(m.sem, m.val, None)

    def final_miles(self):
        return [Mile(s, self.plan.cnt[s], None) for s in self.sems]


def build_program(n_sub=4, debug=False):
    nc = bass.Bass("TRN2", target_bir_lowering=False)
    ctx = ExitStack()
    p = Plan(nc, ctx)

    def din(name, shape, dt):
        return nc.dram_tensor(name, shape, dt, kind="ExternalInput").ap()

    XIN = din("xin", [NG, D, TG], F32)
    WQKV_A = din("wqkv_a", [24, 128, NCH, 256], F32)
    WO_A = din("wo_a", [16, 128, NCH, 256], F32)
    WQ_B = din("wq_b", [8, 128, NCH, 4, 128], F32)
    WKV_B = din("wkv_b", [8, 128, NCH, 256], F32)
    WO_B = din("wo_b", [16, 128, 8, 256], F32)
    WUP = din("wup", [16, 128, 2, 1024], F32)
    WDN = din("wdn", [16, 128, 8, 256], F32)
    PERMC = din("permc", [128, 32], U32)
    CSQ = din("csq", [2, 128, T], F32)
    CSK = din("csk", [2, 128, T], F32)
    MASKC = din("maskc", [128, 128], F32)
    IDENB = din("idenb", [128, 128], BF16)
    ONESCB = din("onescb", [128, 1], BF16)
    ONESR = din("onesr", [1, 128], F32)
    EPSC = din("epsc", [128, 1], F32)
    BIDX = din("bidx", [128, 2], U32)

    XOUT = nc.dram_tensor("xout", [NG, D, TG], F32, kind="ExternalOutput").ap()
    if debug:
        D_ROWS = nc.dram_tensor("d_rows", [2, 128, T], F32, kind="ExternalOutput").ap()
        D_KT = nc.dram_tensor("d_kt", [128, 8 * T], BF16, kind="ExternalOutput").ap()
        D_VT = nc.dram_tensor("d_vt", [128, 8 * T], BF16, kind="ExternalOutput").ap()
        D_QTS = nc.dram_tensor("d_qts", [NG, 32, 128, TG], BF16, kind="ExternalOutput").ap()
        D_OBLK = nc.dram_tensor("d_oblk", [128, 32768], BF16, kind="ExternalOutput").ap()
        D_PB = nc.dram_tensor("d_pb", [128, T], BF16, kind="ExternalOutput").ap()
        D_PTB = nc.dram_tensor("d_ptb", [128, 8192], BF16, kind="ExternalOutput").ap()
        D_COLS = nc.dram_tensor("d_cols", [128, 16], F32, kind="ExternalOutput").ap()
        dbg_st = []
        dbg_att = {"pb": D_PB, "ptb": D_PTB, "cols": D_COLS, "st": dbg_st}
    XA = nc.dram_tensor("XA", [NG, D, TG], F32, kind="Internal").ap()
    XB = nc.dram_tensor("XB", [NG, D, TG], F32, kind="Internal").ap()
    XNS = nc.dram_tensor("XNS", [D, T], BF16, kind="Internal").ap()
    RROWS = nc.dram_tensor("RROWS", [2, T], F32, kind="Internal").ap()
    QTS = nc.dram_tensor("QTS", [NG, 32, 128, TG], BF16, kind="Internal").ap()

    sb = lambda n, s, d: ctx.enter_context(nc.sbuf_tensor(n, s, d))
    rows1 = sb("rows1", [128, T], F32)
    rows2 = sb("rows2", [128, T], F32)
    rmr = sb("rmr", [128, 2048], F32)
    wbuf_u8 = sb("wbuf", [128, 3 * 4096], U8)
    WG = [wbuf_u8[:, i * 4096:(i + 1) * 4096].bitcast(BF16) for i in range(3)]
    maskb = sb("maskb", [128, 128], F32)
    idenb = sb("idenb_s", [128, 128], BF16)
    onescb = sb("onescb_s", [128, 1], BF16)
    onesrb = sb("onesr_s", [1, 128], F32)
    epscb = sb("epsc_s", [128, 1], F32)
    permb = sb("permb", [128, 32], U32)
    bidxb = sb("bidxb", [128, 2], U32)
    colsb = sb("colsb", [128, 16], F32)
    vtmp = sb("vtmp", [128, 1024], BF16)
    pbuf = sb("pbuf", [128, T], BF16)
    bigB_u8 = sb("bigB", [128, 65536], U8)
    KT = sb("KT", [128, 8 * T], BF16)
    VT = sb("VT", [128, 8 * T], BF16)
    qst = sb("qst", [128, 2048], BF16)
    qld = sb("qld", [128, 2048], BF16)
    bigD_u8 = sb("bigD", [128, 24576], U8)

    xn_sb = bigB_u8[:, 0:65536].bitcast(BF16)
    oblk = xn_sb
    ug_sb = bigB_u8[:, 0:32768].bitcast(BF16)
    xg_sb = [bigB_u8[:, 32768 + i * 8192:32768 + (i + 1) * 8192].bitcast(BF16) for i in range(2)]
    xnew_ffn = bigB_u8[:, 49152:65536].bitcast(F32)

    xld = [pbuf[:, :].bitcast(F32), qld[:, :].bitcast(F32)]
    lnscr = bigD_u8[:, 8192:12288].bitcast(F32)
    sqb = qst[:, 0:1024]
    xbfb = qst[:, 1024:2048]
    csb = bigD_u8[:, 12288:20480].bitcast(F32)
    qtmp = bigD_u8[:, 20480:22528].bitcast(F32)
    qswp = bigD_u8[:, 22528:24576].bitcast(F32)
    r3row = bigD_u8[:, 16384:24576].bitcast(F32)[0:1, :]
    ptb = bigD_u8[:, 0:16384].bitcast(BF16)
    xoldb = [bigD_u8[:, i * 8192:(i + 1) * 8192].bitcast(F32) for i in range(2)]
    xnstb = bigD_u8[:, 16384:24576].bitcast(F32)
    xresb = bigD_u8[:, 0:16384].bitcast(F32)
    xnstg = [bigB_u8[:, 49152 + i * 2048:49152 + (i + 1) * 2048].bitcast(BF16) for i in range(2)]

    PS = ctx.enter_context(nc.psum_tensor("PS", [128, 4096], F32))
    PSbf = PS[:, :].bitcast(BF16)

    def bank(i, w=512):
        return PS[:, i * 512:i * 512 + w]

    def bankbf(i, off=0, w=1024):
        return PSbf[:, i * 1024 + off:i * 1024 + off + w]

    wst = Stream(p, "qw", "gps", 3)
    xst = Stream(p, "qx", "sync", 2)
    qsx = Stream(p, "qq", "sync", 2)
    oldst = Stream(p, "qo", "sync", 2)
    gxg = Stream(p, "gg", "gps", 2)
    grs = Stream(p, "gr", "gps", 1)
    stq = Stream(p, "sq", "sync", 2)
    stn = Stream(p, "sn", "sync", 2)
    stx = Stream(p, "sx", "sync", 2)
    stsc = Stream(p, "sc", "gps", 2)
    mld = Stream(p, "ml", "sync", 1)
    qsw = Stream(p, "qv", "sync", 2)
    rrw = Stream(p, "rw", "sync", 2)
    rbc = Stream(p, "rb", "gps", 2)

    wfree = {}
    rmr_guard = [None]
    store_map = {}
    qldfree = {}

    MPE, MVEC, MSCL = "mpe", "mvec", "mscl"
    for s in (MPE, MVEC, MSCL):
        p.sem(s)

    mld.load(lambda e: e.dma_start(out=maskb[:], in_=MASKC))
    mld.load(lambda e: e.dma_start(out=idenb[:], in_=IDENB))
    mld.load(lambda e: e.dma_start(out=onescb[:], in_=ONESCB))
    mld.load(lambda e: e.dma_start(out=onesrb[:], in_=ONESR))
    mld.load(lambda e: e.dma_start(out=epscb[:], in_=EPSC))
    mld.load(lambda e: e.dma_start(out=bidxb[:], in_=BIDX))
    _, consts_ready = mld.load(lambda e: e.dma_start(out=permb[:], in_=PERMC))

    def stats_pass(src, deps=(), chunk_deps=None):
        r1 = rows1[0:1, :]
        r2 = rows2[0:1, :]
        r3 = r3row
        pe_mile = None
        buf_guard = {}
        dep_list = list(deps) + [consts_ready]
        for g in range(NG):
            for cc in range(NCH):
                cdep = list(chunk_deps[(g, cc)]) if chunk_deps is not None else dep_list
                par, lm = xst.load(
                    lambda e, g=g, cc=cc, b=xst.i % 2: e.dma_start(
                        out=xld[b][:], in_=src[g, cc * 128:(cc + 1) * 128, :]),
                    waits=cdep + [buf_guard.get(xst.i % 2)])
                sm = p.emit("scl", lambda e, par=par: e.activation(
                    sqb[:], xld[par][:], AF.Square), waits=[lm, pe_mile], inc=MSCL)
                vm = p.emit("vec", lambda e, par=par: e.tensor_copy(
                    xbfb[:], xld[par][:]), waits=[lm, pe_mile], inc=MVEC)
                for tq in range(2):
                    for s_i, src_sb in ((0, xbfb), (1, sqb)):
                        bk = g * 4 + tq * 2 + s_i
                        pe_mile = p.emit("pe", lambda e, bk=bk, tq=tq, src_sb=src_sb, cc=cc: e.matmul(
                            PS[0:1, bk * 512:(bk + 1) * 512],
                            onescb[:],
                            src_sb[:, tq * 512:(tq + 1) * 512],
                            start=(cc == 0), stop=(cc == NCH - 1)),
                            waits=[sm, vm], inc=MPE)
                buf_guard[par] = pe_mile
        ev = None
        for g in range(NG):
            for tq in range(2):
                for s_i, dst in ((0, r1), (1, r2)):
                    bk = g * 4 + tq * 2 + s_i
                    t0 = g * 1024 + tq * 512
                    ev = p.emit("vec", lambda e, bk=bk, dst=dst, t0=t0: e.tensor_copy(
                        dst[:, t0:t0 + 512], PS[0:1, bk * 512:(bk + 1) * 512]),
                        waits=[pe_mile], inc=MVEC)
        ev = p.emit("vec", lambda e: e.tensor_scalar_mul(r1, r1, 1.0 / D),
                    waits=[ev], hard_waits=[ev], inc=MVEC)
        ev = p.emit("vec", lambda e: e.tensor_scalar_mul(r2, r2, 1.0 / D),
                    waits=[], hard_waits=[ev], inc=MVEC)
        ev = p.emit("vec", lambda e: e.tensor_mul(r3, r1, r1),
                    waits=list(deps), hard_waits=[ev], inc=MVEC)
        ev = p.emit("vec", lambda e: e.tensor_sub(r2, r2, r3),
                    waits=[], hard_waits=[ev], inc=MVEC)
        sm = p.emit("scl", lambda e: e.activation(r3, r2, AF.Sqrt, bias=epscb[0:1, :]),
                    waits=[ev, consts_ready], inc=MSCL)
        ev = p.emit("vec", lambda e: e.reciprocal(r2, r3), waits=[sm], inc=MVEC)
        ev = p.emit("vec", lambda e: e.tensor_mul(r1, r1, r2),
                    waits=[], hard_waits=[ev], inc=MVEC)
        _, s1m = rrw.load(lambda e: e.dma_start(out=RROWS[0], in_=rows1[0:1, :]),
                          waits=[ev])
        _, s2m = rrw.load(lambda e: e.dma_start(out=RROWS[1], in_=rows2[0:1, :]),
                          waits=[ev])
        return [s1m, s2m]

    def build_rmr(g, rows_ready):
        guard = rmr_guard[0]
        ms = []
        for dst_off, row in ((0, 1), (1024, 0)):
            _, m = rbc.load(lambda e, g=g, dst_off=dst_off, row=row: e.indirect_dma_start(
                rmr[:, dst_off:dst_off + 1024], None,
                RROWS,
                bass.IndirectOffsetOnAxis(ap=bidxb[:, row:row + 1], axis=0),
                element_offset=g * 1024),
                waits=list(rows_ready) + [consts_ready, guard])
            ms.append(m)
        return ms

    def ln_chunk(par, lm, dst_ap, rmr_ready, extra=()):
        v1 = p.emit("vec", lambda e, par=par: e.tensor_mul(
            lnscr[:], xld[par][:], rmr[:, 0:1024]),
            waits=[lm] + list(rmr_ready) + list(extra), inc=MVEC)
        v2 = p.emit("vec", lambda e, dst_ap=dst_ap: e.tensor_sub(
            dst_ap, lnscr[:], rmr[:, 1024:2048]), waits=[v1], inc=MVEC)
        return v2

    def ln_phase(src, rows_ready, g, group_guard, to_xns=False, store_guard=None):
        if to_xns and os.environ.get("LN_NORMR", "0") == "1":
            rmr_ready = rows_ready
        else:
            rmr_ready = build_rmr(g, rows_ready)
        last_ln = {}
        miles = []
        stores = []
        ncc_knob = int(os.environ.get("LN_NCC", "32")) if to_xns else NCH
        for cc in range(ncc_knob):
            par, lm = xst.load(
                lambda e, g=g, cc=cc, src=src, b=xst.i % 2: e.dma_start(
                    out=xld[b][:], in_=src[g, cc * 128:(cc + 1) * 128, :]),
                waits=[last_ln.get(xst.i % 2)] + list(rows_ready))
            if to_xns:
                spar = cc % 2
                v = ln_chunk(par, lm, xnstg[spar][:], rmr_ready,
                             extra=[group_guard, store_guard[spar]])
                if os.environ.get("FFN_NOST", "0") != "1":
                    _, smn = stn.load(lambda e, cc=cc, g=g, spar=spar: e.dma_start(
                        out=XNS[cc * 128:(cc + 1) * 128, g * 1024:(g + 1) * 1024],
                        in_=xnstg[spar][:]), waits=[v])
                    store_guard[spar] = smn
                    stores.append(smn)
            else:
                v = ln_chunk(par, lm, xn_sb[:, cc * 1024:(cc + 1) * 1024], rmr_ready,
                             extra=[group_guard])
            last_ln[par] = v
            miles.append(v)
        rmr_guard[0] = miles[-1] if miles else rmr_guard[0]
        return miles, stores

    def proj_pass_a(src, rows_ready):
        wo_free = wfree
        qst_free = [None, None]
        vt_free = [None, None]
        evm = {}
        prev_rope = [None, None]
        group_guard = None
        for g in range(NG):
            xn_mile, _ = ln_phase(src, rows_ready, g, group_guard)
            for hf in range(2):
                for tb in range(2):
                    _, cs_ready = mld.load(lambda e, g=g, hf=hf, tb=tb: e.dma_start(
                        out=csb[:, (hf * 2 + tb) * 512:(hf * 2 + tb) * 512 + 512],
                        in_=CSQ[tb, :, g * 1024 + hf * 512:g * 1024 + hf * 512 + 512]),
                        waits=[evm.get("lastk")])
            csk_loaded = False
            mm_done = None
            for pr in range(24):
                kind = "q" if pr < 16 else ("k" if pr < 20 else "v")
                pi = (pr % 2) if kind != "v" else 0
                bank0 = pi * 4
                if kind == "k" and not csk_loaded:
                    for hf in range(2):
                        for tb in range(2):
                            _, cs_ready = mld.load(lambda e, g=g, hf=hf, tb=tb: e.dma_start(
                                out=csb[:, (hf * 2 + tb) * 512:(hf * 2 + tb) * 512 + 512],
                                in_=CSK[tb, :, g * 1024 + hf * 512:g * 1024 + hf * 512 + 512]),
                                waits=[evm.get("lastq")])
                    csk_loaded = True
                if kind == "v":
                    v_guard = [evm.get((1, j, hf)) for j in range(2) for hf in range(2)]
                else:
                    v_guard = []
                for q8 in range(4):
                    wb, wm = wst.load(
                        lambda e, pr=pr, q8=q8, b=wst.i % 3: e.dma_start(
                            out=WG[b][:], in_=WQKV_A[pr, :, q8 * 8:(q8 + 1) * 8, :]),
                        waits=[wo_free.get(wst.i % 3)])
                    for c8 in range(8):
                        cc = q8 * 8 + c8
                        for j in range(2):
                            for hf in range(2):
                                bk = bank0 + j * 2 + hf
                                mm_done = p.emit("pe", lambda e, wb=wb, c8=c8, j=j, hf=hf, bk=bk, cc=cc: e.matmul(
                                    bank(bk),
                                    WG[wb][:, (c8 * 2 + j) * 128:(c8 * 2 + j) * 128 + 128],
                                    xn_sb[:, cc * 1024 + hf * 512:cc * 1024 + hf * 512 + 512],
                                    start=(cc == 0), stop=(cc == NCH - 1)),
                                    waits=[wm, xn_mile[cc], evm.get((pi, j, hf))], inc=MPE)
                    wo_free[wb] = mm_done
                for j in range(2):
                    ch = pr * 2 + j
                    for hf in range(2):
                        bk = bank0 + j * 2 + hf
                        if kind in ("q", "k"):
                            if kind == "q":
                                sl = (ch % 2) * 1024 + hf * 512
                                dst = qst[:, sl:sl + 512]
                                extra = [qst_free[ch % 2]]
                            else:
                                kv = ch - 32
                                base = kv * T + g * 1024 + hf * 512
                                dst = KT[:, base:base + 512]
                                extra = []
                            vc = p.emit("vec", lambda e, bk=bk: e.tensor_copy(
                                qtmp[:], bank(bk)), waits=[mm_done, prev_rope[0]] + extra, inc=MVEC)
                            _, md1 = qsw.load(lambda e: e.dma_start(
                                out=qswp[0:64, :], in_=qtmp[64:128, :]), waits=[vc])
                            _, md2 = qsw.load(lambda e: e.dma_start(
                                out=qswp[64:128, :], in_=qtmp[0:64, :]), waits=[vc])
                            m1 = p.emit("vec", lambda e, hf=hf: e.tensor_mul(
                                qtmp[:], qtmp[:], csb[:, (hf * 2 + 0) * 512:(hf * 2 + 0) * 512 + 512]),
                                waits=[md1, md2, cs_ready], inc=MVEC)
                            m2 = p.emit("vec", lambda e, hf=hf: e.tensor_mul(
                                qswp[:], qswp[:], csb[:, (hf * 2 + 1) * 512:(hf * 2 + 1) * 512 + 512]),
                                waits=[m1], inc=MVEC)
                            v1 = p.emit("vec", lambda e, dst=dst: e.tensor_add(
                                dst, qtmp[:], qswp[:]), waits=[m2], inc=MVEC)
                            prev_rope[0] = v1
                            evm[(pi, j, hf)] = vc
                            if kind == "q":
                                evm["lastq"] = v1
                                if hf == 1:
                                    _, smq = stq.load(lambda e, ch=ch, g=g: e.dma_start(
                                        out=QTS[g, ch], in_=qst[:, (ch % 2) * 1024:(ch % 2) * 1024 + 1024]),
                                        waits=[prev_rope[1], v1])
                                    qst_free[ch % 2] = smq
                                    prev_rope[1] = None
                                else:
                                    prev_rope[1] = v1
                            else:
                                evm["lastk"] = v1
                        else:
                            kv = ch - 40
                            vp = hf
                            sc_copy = p.emit("scl", lambda e, bk=bk, vp=vp: e.copy(
                                vtmp[:, vp * 512:vp * 512 + 512], bank(bk)),
                                waits=[mm_done, vt_free[vp]], inc=MSCL)
                            trs = None
                            for k in range(4):
                                trs = p.emit("pe", lambda e, k=k, vp=vp: e.transpose(
                                    bankbf(4, k * 128, 128),
                                    vtmp[:, vp * 512 + k * 128:vp * 512 + (k + 1) * 128],
                                    idenb[:]),
                                    waits=[sc_copy, evm.get("vtev")] + v_guard, inc=MPE)
                            sc0 = g * 8 + hf * 4
                            evv = p.emit("vec", lambda e, sc0=sc0, kv=kv: e.tensor_copy(
                                VT[:, :].rearrange("p (sc u d) -> p sc u d", sc=16, u=8)[:, sc0:sc0 + 4, kv, :],
                                bankbf(4, 0, 512).rearrange("p (k d) -> p k d", k=4)),
                                waits=[trs], inc=MVEC)
                            evm["vtev"] = evv
                            vt_free[vp] = evv
                            evm[(0, j, hf)] = evv
            group_guard = mm_done
        return qst_free

    def proj_pass_b(src, rows_ready):
        wo_free = wfree
        qst_free = [None, None]
        vt_free = [None, None]
        evm = {}
        bkg = {}
        group_guard = None
        for g in range(NG):
            xn_mile, _ = ln_phase(src, rows_ready, g, group_guard)
            mm_done = None
            for u in range(8):
                pi = u % 2
                for gr in range(8):
                    wb, wm = wst.load(
                        lambda e, u=u, gr=gr, b=wst.i % 3: e.dma_start(
                            out=WG[b][:], in_=WQ_B[u, :, gr * 4:(gr + 1) * 4, :, :]),
                        waits=[wo_free.get(wst.i % 3)])
                    for c4 in range(4):
                        cc = gr * 4 + c4
                        for gg in range(4):
                            for hf in range(2):
                                bk = pi * 2 + hf
                                mm_done = p.emit("pe", lambda e, wb=wb, c4=c4, gg=gg, hf=hf, bk=bk, cc=cc: e.matmul(
                                    bank(bk),
                                    WG[wb][:, (c4 * 4 + gg) * 128:(c4 * 4 + gg) * 128 + 128],
                                    xn_sb[:, cc * 1024 + hf * 512:cc * 1024 + hf * 512 + 512],
                                    start=(cc == 0 and gg == 0), stop=(cc == NCH - 1 and gg == 3)),
                                    waits=[wm, xn_mile[cc], bkg.get(bk)], inc=MPE)
                    wo_free[wb] = mm_done
                for hf in range(2):
                    bk = pi * 2 + hf
                    sc = p.emit("scl", lambda e, bk=bk, u=u, hf=hf: e.activation(
                        qst[:, (u % 2) * 1024 + hf * 512:(u % 2) * 1024 + hf * 512 + 512],
                        bank(bk), AF.Copy, scale=ISQ),
                        waits=[mm_done, qst_free[u % 2]], inc=MSCL)
                    bkg[bk] = sc
                _, smq = stq.load(lambda e, u=u, g=g: e.dma_start(
                    out=QTS[g, u], in_=qst[:, (u % 2) * 1024:(u % 2) * 1024 + 1024]),
                    waits=[bkg[pi * 2], bkg[pi * 2 + 1]])
                qst_free[u % 2] = smq

            for pr in range(8):
                kind = "k" if pr < 4 else "v"
                pi = pr % 2
                bank0 = (pi * 4) if kind == "k" else 4
                v_guard = [bkg.get(0)] if kind == "v" else []
                mm_done = None
                for q8 in range(4):
                    wb, wm = wst.load(
                        lambda e, pr=pr, q8=q8, b=wst.i % 3: e.dma_start(
                            out=WG[b][:], in_=WKV_B[pr, :, q8 * 8:(q8 + 1) * 8, :]),
                        waits=[wo_free.get(wst.i % 3)])
                    for c8 in range(8):
                        cc = q8 * 8 + c8
                        for j in range(2):
                            for hf in range(2):
                                bk = bank0 + j * 2 + hf
                                mm_done = p.emit("pe", lambda e, wb=wb, c8=c8, j=j, hf=hf, bk=bk, cc=cc: e.matmul(
                                    bank(bk),
                                    WG[wb][:, (c8 * 2 + j) * 128:(c8 * 2 + j) * 128 + 128],
                                    xn_sb[:, cc * 1024 + hf * 512:cc * 1024 + hf * 512 + 512],
                                    start=(cc == 0), stop=(cc == NCH - 1)),
                                    waits=[wm, xn_mile[cc], bkg.get(bk)], inc=MPE)
                    wo_free[wb] = mm_done
                for j in range(2):
                    ch = pr * 2 + j
                    for hf in range(2):
                        bk = bank0 + j * 2 + hf
                        if kind == "k":
                            kv = ch
                            base = kv * T + g * 1024 + hf * 512
                            evk = p.emit("scl", lambda e, bk=bk, base=base: e.copy(
                                KT[:, base:base + 512], bank(bk)),
                                waits=[mm_done], inc=MSCL)
                            bkg[bk] = evk
                        else:
                            kv = ch - 8
                            vp = hf
                            sc_copy = p.emit("scl", lambda e, bk=bk, vp=vp: e.copy(
                                vtmp[:, vp * 512:vp * 512 + 512], bank(bk)),
                                waits=[mm_done, vt_free[vp]], inc=MSCL)
                            bkg[bk] = sc_copy
                            trs = None
                            for k in range(4):
                                trs = p.emit("pe", lambda e, k=k, vp=vp: e.transpose(
                                    bankbf(0, k * 128, 128),
                                    vtmp[:, vp * 512 + k * 128:vp * 512 + (k + 1) * 128],
                                    idenb[:]),
                                    waits=[sc_copy, evm.get("vtev")] + v_guard, inc=MPE)
                            sc0 = g * 8 + hf * 4
                            evv = p.emit("vec", lambda e, sc0=sc0, kv=kv: e.tensor_copy(
                                VT[:, :].rearrange("p (sc u d) -> p sc u d", sc=16, u=8)[:, sc0:sc0 + 4, kv, :],
                                bankbf(0, 0, 512).rearrange("p (k d) -> p k d", k=4)),
                                waits=[trs], inc=MVEC)
                            evm["vtev"] = evv
                            vt_free[vp] = evv
                            bkg[0] = evv
            group_guard = mm_done
        return qst_free

    def attention_block(blk, variant, q_ready, oblk_free, ptb_free, dbg=None):
        heads = 32 if variant == "a" else 8
        qld_free = qldfree
        last_o_ev = None
        exp_prev = oblk_free
        trs_prev = None
        pt_ev_prev = None
        pv_last = ptb_free
        for hi in range(heads):
            kv = hi // 4 if variant == "a" else hi
            h = hi
            qb, qm = qsx.load(lambda e, h=h, blk=blk, b=qsx.i % 2: e.dma_start(
                out=qld[:, b * 1024:b * 1024 + 1024], in_=QTS[blk, h]),
                waits=list(q_ready) + [qld_free.get(qsx.i % 2)])
            for sub in range(2):
                qbase = blk * 1024 + sub * 512
                nsc_full = 16 if variant == "b" else (qbase + 512) // 128
                first_ev_guard = pv_last
                if variant == "a":
                    dsc0 = qbase // 128
                    pt_ev_prev = p.emit("vec", lambda e, dsc0=dsc0: e.memset(
                        ptb.rearrange("p (sc q) -> p sc q", sc=16)[:, dsc0:dsc0 + 4, :], 0.0),
                        waits=[pv_last, pt_ev_prev], inc=MVEC)
                    first_ev_guard = None
                for qt4 in range(4):
                    qt = sub * 4 + qt4
                    qabs = blk * 8 + qt
                    slen = (qabs + 1) * 128 if variant == "a" else T
                    nS = (slen + 511) // 512
                    par = qt % 2
                    sm_miles = []
                    for sci in range(nS):
                        w = min(512, slen - sci * 512)
                        sm = p.emit("pe", lambda e, qb=qb, qt=qt, kv=kv, sci=sci, w=w: e.matmul(
                            PS[:, sci * 512:sci * 512 + w],
                            qld[:, qb * 1024 + qt * 128:qb * 1024 + qt * 128 + 128],
                            KT[:, kv * T + sci * 512:kv * T + sci * 512 + w],
                            start=True, stop=True),
                            waits=[qm, exp_prev], inc=MPE)
                        sm_miles.append(sm)
                    sm_last = sm_miles[-1]
                    if variant == "a":
                        vmk = p.emit("vec", lambda e, slen=slen: e.tensor_add(
                            PS[:, slen - 128:slen], PS[:, slen - 128:slen], maskb[:]),
                            waits=[sm_last], inc=MVEC)
                    else:
                        vmk = sm_last
                    vmx = None
                    for sci in range(nS):
                        w = min(512, slen - sci * 512)
                        dst_sl = colsb[:, par * 4:par * 4 + 1] if sci == 0 else colsb[:, par * 4 + 3:par * 4 + 4]
                        wt = [sm_miles[sci]] if sci < nS - 1 else [vmk]
                        vm_c = p.emit("vec", lambda e, sci=sci, w=w, dst_sl=dst_sl: e.reduce_max(
                            dst_sl, PS[:, sci * 512:sci * 512 + w], AX.X, negate=True),
                            waits=wt, inc=MVEC)
                        if sci == 0:
                            vmx = vm_c
                        else:
                            vmx = p.emit("vec", lambda e, par=par: e.tensor_tensor(
                                colsb[:, par * 4:par * 4 + 1],
                                colsb[:, par * 4:par * 4 + 1],
                                colsb[:, par * 4 + 3:par * 4 + 4], ALU.min),
                                waits=[], hard_waits=[vm_c], inc=MVEC)
                    exp_m = p.emit("scl", lambda e, slen=slen, par=par: e.activation(
                        pbuf[:, 0:slen], PS[:, 0:slen], AF.Exp,
                        bias=colsb[:, par * 4:par * 4 + 1],
                        accum_out=colsb[:, par * 4 + 1:par * 4 + 2]),
                        waits=[vmx, trs_prev], inc=MSCL)
                    exp_prev = exp_m
                    vrc = p.emit("vec", lambda e, par=par: e.reciprocal(
                        colsb[:, par * 4 + 2:par * 4 + 3], colsb[:, par * 4 + 1:par * 4 + 2]),
                        waits=[exp_m], inc=MVEC)
                    vnm = p.emit("vec", lambda e, slen=slen, par=par: e.tensor_scalar_mul(
                        pbuf[:, 0:slen], pbuf[:, 0:slen], colsb[:, par * 4 + 2:par * 4 + 3]),
                        waits=[], hard_waits=[vrc], inc=MVEC)
                    n128 = slen // 128
                    trs = None
                    for sc in range(n128):
                        trs = p.emit("pe", lambda e, sc=sc: e.transpose(
                            PSbf[:, 4 * 1024 + sc * 128:4 * 1024 + (sc + 1) * 128],
                            pbuf[:, sc * 128:(sc + 1) * 128], idenb[:]),
                            waits=[vnm, pt_ev_prev], inc=MPE)
                    trs_prev = trs
                    pt_ev_prev = p.emit("vec", lambda e, n128=n128, qt4=qt4: e.tensor_copy(
                        ptb.rearrange("p (sc q) -> p sc q", sc=16)[:, 0:n128, qt4 * 128:(qt4 + 1) * 128],
                        PSbf[:, 4 * 1024:4 * 1024 + n128 * 128].rearrange("p (sc d) -> p sc d", d=128)),
                        waits=[trs, first_ev_guard], inc=MVEC)
                    first_ev_guard = None
                    if dbg is not None and hi == 0 and blk == 0 and sub == 0 and qt4 == 3:
                        d1 = p.emit("sync", lambda e: e.dma_start(
                            out=dbg["pb"], in_=pbuf[:]), waits=[vnm], inc="dbg", amt=16)
                        d2 = p.emit("sync", lambda e: e.dma_start(
                            out=dbg["cols"], in_=colsb[:]), waits=[vnm], inc="dbg", amt=16)
                        d3 = p.emit("sync", lambda e: e.dma_start(
                            out=dbg["ptb"], in_=ptb[:]), waits=[pt_ev_prev], inc="dbg", amt=16)
                        bl = Mile("dbg", p.cnt["dbg"], None)
                        dbg["st"] += [d1, d2, d3]
                        exp_prev = bl
                        pt_ev_prev = bl
                        trs_prev = bl
                pv_last = None
                for sc in range(nsc_full):
                    pv_last = p.emit("pe", lambda e, sc=sc, kv=kv, first=(sc == 0), last=(sc == nsc_full - 1): e.matmul(
                        bank(6),
                        VT[:, (sc * 8 + kv) * 128:(sc * 8 + kv) * 128 + 128],
                        ptb[:, sc * 512:(sc + 1) * 512],
                        start=first, stop=last),
                        waits=[pt_ev_prev, last_o_ev], inc=MPE)
                last_o_ev = p.emit("vec", lambda e, h=h, sub=sub: e.tensor_copy(
                    oblk[:, h * 1024 + sub * 512:h * 1024 + sub * 512 + 512], bank(6)),
                    waits=[pv_last, oblk_free], inc=MVEC)
            qld_free[qb] = last_o_ev
        return last_o_ev, pv_last

    def wo_epilogue(blk, variant, o_ready, src, dst):
        ncc = 32 if variant == "a" else 8
        wsrc = WO_A if variant == "a" else WO_B
        ngr = 4 if variant == "a" else 1
        wo_free = wfree
        epi_free = {}
        xold_free = [None, None]
        xnst_free = None
        last_store = None
        for pr in range(16):
            opar, om = oldst.load(lambda e, pr=pr, blk=blk, src=src, b=oldst.i % 2: e.dma_start(
                out=xoldb[b][:].rearrange("p (c t) -> p c t", c=2),
                in_=src[blk, pr * 256:(pr + 1) * 256, :].rearrange("(c p) t -> p c t", p=128)),
                waits=[xold_free[oldst.i % 2], o_ready])
            mm_done = None
            b0 = (pr % 2) * 4
            for gri in range(ngr):
                wb, wm = wst.load(
                    lambda e, pr=pr, gri=gri, wsrc=wsrc, b=wst.i % 3: e.dma_start(
                        out=WG[b][:], in_=wsrc[pr, :, gri * 8:(gri + 1) * 8, :]),
                    waits=[wo_free.get(wst.i % 3)])
                for c8 in range(8):
                    cc = gri * 8 + c8
                    if cc >= ncc:
                        break
                    for j in range(2):
                        for hf in range(2):
                            bk = b0 + j * 2 + hf
                            mm_done = p.emit("pe", lambda e, wb=wb, c8=c8, j=j, hf=hf, bk=bk, cc=cc: e.matmul(
                                bank(bk),
                                WG[wb][:, (c8 * 2 + j) * 128:(c8 * 2 + j) * 128 + 128],
                                oblk[:, cc * 1024 + hf * 512:cc * 1024 + hf * 512 + 512],
                                start=(cc == 0), stop=(cc == ncc - 1)),
                                waits=[wm, o_ready, epi_free.get(bk)], inc=MPE)
                wo_free[wb] = mm_done
            adds = []
            for j in range(2):
                for hf in range(2):
                    bk = b0 + j * 2 + hf
                    va = p.emit("vec", lambda e, bk=bk, j=j, hf=hf, opar=opar: e.tensor_add(
                        xnstb[:, j * 1024 + hf * 512:j * 1024 + hf * 512 + 512],
                        bank(bk),
                        xoldb[opar][:, j * 1024 + hf * 512:j * 1024 + hf * 512 + 512]),
                        waits=[mm_done, om, xnst_free], inc=MVEC)
                    epi_free[bk] = va
                    adds.append(va)
            xold_free[opar] = adds[-1]
            _, smx = stx.load(lambda e, pr=pr, blk=blk, dst=dst: e.dma_start(
                out=dst[blk, pr * 256:(pr + 1) * 256, :].rearrange("(c p) t -> p c t", p=128),
                in_=xnstb[:].rearrange("p (c t) -> p c t", c=2)),
                waits=adds)
            xnst_free = smx
            last_store = smx
            store_map[(blk, pr * 2)] = [smx]
            store_map[(blk, pr * 2 + 1)] = [smx]
        return last_store

    def ffn_pass(src, dst, rows_ready, prev_guard):
        xresv = [KT[:, :].bitcast(F32)[:, i * 4096:(i + 1) * 4096] for i in range(2)]
        rmr_f = VT[:, :].bitcast(F32)[:, 0:4096]
        rmrf_m = []
        for g in range(NG):
            for dst_off, row in ((g * 1024, 1), (2048 + g * 1024, 0)):
                _, m = rbc.load(lambda e, dst_off=dst_off, row=row, g=g: e.indirect_dma_start(
                    rmr_f[:, dst_off:dst_off + 1024], None,
                    RROWS,
                    bass.IndirectOffsetOnAxis(ap=bidxb[:, row:row + 1], axis=0),
                    element_offset=g * 1024),
                    waits=list(rows_ready) + [consts_ready] + list(prev_guard))
                rmrf_m.append(m)

        wo_free = wfree
        xres_free = [None, None]
        xg_free = [None, None]
        up_free = {}
        dn_free = {}
        ug_free = {}
        prev_scat = []
        nh = int(os.environ.get("FFN_NH", "16"))

        def issue_loads(h):
            hp = h % 2
            rms = []
            for k in range(2):
                for g in range(NG):
                    _, rm = grs.load(lambda e, h=h, k=k, g=g, hp=hp: e.indirect_dma_start(
                        xresv[hp][:, (k * 2 + g) * 1024:(k * 2 + g + 1) * 1024], None,
                        src.rearrange("g d t -> (g d) t"),
                        bass.IndirectOffsetOnAxis(ap=permb[:, h * 2 + k:h * 2 + k + 1], axis=0),
                        element_offset=g * D * TG),
                        waits=[xres_free[hp], consts_ready] + list(rows_ready) + list(prev_guard))
                    rms.append(rm)
            wbu, wmu = wst.load(lambda e, h=h, b=wst.i % 3: e.dma_start(
                out=WG[b][:], in_=WUP[h]), waits=[wo_free.get(wst.i % 3)])
            return dict(rms=rms, wbu=wbu, wmu=wmu, hp=hp)

        def issue_dn(h):
            wbd, wmd = wst.load(lambda e, h=h, b=wst.i % 3: e.dma_start(
                out=WG[b][:], in_=WDN[h]), waits=[wo_free.get(wst.i % 3)])
            return wbd, wmd

        pend = issue_loads(0) if nh > 0 else None
        if nh > 0:
            pend["wbd"], pend["wmd"] = issue_dn(0)
        for h in range(nh):
            cur = pend
            pend = issue_loads(h + 1) if h + 1 < nh else None
            hp = cur["hp"]
            ttm = {}
            for dc in range(2):
                for g in range(NG):
                    v1 = p.emit("vec", lambda e, dc=dc, g=g, hp=hp: e.tensor_mul(
                        lnscr[:], xresv[hp][:, (dc * 2 + g) * 1024:(dc * 2 + g + 1) * 1024],
                        rmr_f[:, g * 1024:g * 1024 + 1024]),
                        waits=cur["rms"] + rmrf_m + [xg_free[hp]], inc=MVEC)
                    v2 = p.emit("vec", lambda e, dc=dc, g=g, hp=hp: e.tensor_sub(
                        xg_sb[hp][:, dc * 2048 + g * 1024:dc * 2048 + (g + 1) * 1024],
                        lnscr[:], rmr_f[:, 2048 + g * 1024:2048 + g * 1024 + 1024]),
                        waits=[v1], inc=MVEC)
                    ttm[(dc, g)] = v2
            mmu = {}
            mm_last_up = None
            for tq in range(4):
                for m4 in range(2):
                    pi = (tq * 2 + m4) % 2
                    last = None
                    for mi in range(4):
                        mc = m4 * 4 + mi
                        for dc in range(2):
                            last = p.emit("pe", lambda e, mc=mc, dc=dc, tq=tq, pi=pi, mi=mi, hp=hp, wbu=cur["wbu"]: e.matmul(
                                bank(pi * 4 + mi),
                                WG[wbu][:, dc * 1024 + mc * 128:dc * 1024 + mc * 128 + 128],
                                xg_sb[hp][:, dc * 2048 + tq * 512:dc * 2048 + tq * 512 + 512],
                                start=(dc == 0), stop=(dc == 1)),
                                waits=[cur["wmu"], ttm[(0, tq // 2)], ttm[(1, tq // 2)],
                                       up_free.get((pi, mi))], inc=MPE)
                    mm_last_up = last
                    for mi in range(4):
                        mc = m4 * 4 + mi
                        gv = p.emit("scl", lambda e, pi=pi, mi=mi, mc=mc, tq=tq: e.activation(
                            ug_sb[:, mc * 2048 + tq * 512:mc * 2048 + tq * 512 + 512],
                            bank(pi * 4 + mi), AF.Gelu),
                            waits=[last, ug_free.get((mc, tq))], inc=MSCL)
                        up_free[(pi, mi)] = gv
                        mmu[(mc, tq)] = gv
            wo_free[cur["wbu"]] = mm_last_up
            xg_free[hp] = mm_last_up
            gel_all = p.now(MSCL, "scl")
            dn_free[0] = gel_all
            dn_free[1] = gel_all
            h_adds = []
            for tq in range(4):
                g, hf = tq // 2, tq % 2
                for j2 in range(2):
                    bk = j2
                    last = None
                    for mc in range(8):
                        last = p.emit("pe", lambda e, mc=mc, j2=j2, tq=tq, bk=bk, wbd=cur["wbd"]: e.matmul(
                            bank(bk),
                            WG[wbd][:, mc * 256 + j2 * 128:mc * 256 + j2 * 128 + 128],
                            ug_sb[:, mc * 2048 + tq * 512:mc * 2048 + tq * 512 + 512],
                            start=(mc == 0), stop=(mc == 7)),
                            waits=[cur["wmd"], mmu[(mc, tq)], dn_free.get(bk)], inc=MPE)
                    va = p.emit("vec", lambda e, bk=bk, j2=j2, g=g, hf=hf, hp=hp: e.tensor_add(
                        xnew_ffn[:, (j2 * 2 + g) * 1024 + hf * 512:(j2 * 2 + g) * 1024 + hf * 512 + 512],
                        bank(bk),
                        xresv[hp][:, (j2 * 2 + g) * 1024 + hf * 512:(j2 * 2 + g) * 1024 + hf * 512 + 512]),
                        waits=[last] + cur["rms"] + prev_scat, inc=MVEC)
                    dn_free[bk] = va
                    h_adds.append(va)
            wo_free[cur["wbd"]] = h_adds[-1]
            prev_scat = []
            for k in range(2):
                for g in range(NG):
                    _, scm = stsc.load(lambda e, h=h, k=k, g=g, dst=dst: e.indirect_dma_start(
                        dst.rearrange("g d t -> (g d) t"),
                        bass.IndirectOffsetOnAxis(ap=permb[:, h * 2 + k:h * 2 + k + 1], axis=0),
                        xnew_ffn[:, (k * 2 + g) * 1024:(k * 2 + g + 1) * 1024], None,
                        element_offset=g * D * TG),
                        waits=h_adds)
                    prev_scat.append(scm)
            prev_scat = prev_scat[-2:]
            xres_free[hp] = h_adds[-1]
            for mi in range(4):
                up_free[(0, mi)] = h_adds[-1]
            for tq in range(4):
                for mc in range(8):
                    ug_free[(mc, tq)] = h_adds[-1]
            if pend is not None:
                pend["wbd"], pend["wmd"] = issue_dn(h + 1)
        return stsc.final_miles()

    seq = [("a", XIN, XA), ("f", XA, XB), ("b", XB, XA), ("f", XA, XOUT)]
    seq = seq[:n_sub]
    kind0, s0, _ = seq[-1]
    seq[-1] = (kind0, s0, XOUT)

    prev_done = ()
    final_stream = None
    first = True
    prev_kind = None
    for kind, src, dst in seq:
        cd = dict(store_map) if prev_kind in ("a", "b") else None
        rows_ready = stats_pass(src, deps=prev_done, chunk_deps=cd)
        prev_kind = kind
        if debug and first:
            dbg_st.append(p.emit("sync", lambda e: e.dma_start(out=D_ROWS[0], in_=rows1[:]),
                                 waits=list(rows_ready), inc="dbg", amt=16))
            dbg_st.append(p.emit("sync", lambda e: e.dma_start(out=D_ROWS[1], in_=rows2[:]),
                                 waits=list(rows_ready), inc="dbg", amt=16))
        if kind in ("a", "b"):
            if kind == "a":
                qfree = proj_pass_a(src, rows_ready)
            else:
                qfree = proj_pass_b(src, rows_ready)
            q_ready = [m for m in qfree if m is not None]
            vec_now = p.now(MVEC, "vec")
            scl_now = p.now(MSCL, "scl")
            pe_now = p.now(MPE, "pe")
            p.barrier_wait("pe", [vec_now, scl_now])
            if debug and first:
                dbg_st.append(p.emit("sync", lambda e: e.dma_start(out=D_KT, in_=KT[:]),
                                     waits=[vec_now, scl_now], inc="dbg", amt=16))
                dbg_st.append(p.emit("sync", lambda e: e.dma_start(out=D_VT, in_=VT[:]),
                                     waits=[vec_now, scl_now], inc="dbg", amt=16))
                for gdb in range(NG):
                    for hdb in range(32):
                        dbg_st.append(p.emit("sync", lambda e, gdb=gdb, hdb=hdb: e.dma_start(
                            out=D_QTS[gdb, hdb], in_=QTS[gdb, hdb]),
                            waits=[vec_now] + q_ready, inc="dbg", amt=16))
            stores = []
            store_map.clear()
            oblk_free = vec_now
            ptb_free = None
            o_ready = None
            for blk in range(2):
                o_ready, pv_last = attention_block(blk, kind, q_ready, oblk_free, ptb_free,
                                                   dbg=dbg_att if (debug and first) else None)
                if debug and first and blk == 0:
                    dbg_st.append(p.emit("sync", lambda e: e.dma_start(out=D_OBLK, in_=oblk[:, 0:32768]),
                                         waits=[o_ready], inc="dbg", amt=16))
                stv = wo_epilogue(blk, kind, o_ready, src, dst)
                stores.append(stv)
                oblk_free = stv
                ptb_free = stv
            attn_done = p.now(MVEC, None)
            for k2 in list(store_map):
                store_map[k2] = store_map[k2][:1] + [attn_done]
            prev_done = tuple(stores) + tuple(stx.final_miles())
            final_stream = stx
        else:
            sc = ffn_pass(src, dst, rows_ready, prev_done)
            prev_done = tuple(sc)
            final_stream = stsc

        first = False
    final_miles = list(prev_done) + final_stream.final_miles()
    if debug:
        final_miles += [Mile("dbg", p.cnt.get("dbg", 0), None)]
    p.barrier_wait("sync", final_miles)
    p.barrier_wait("gps", final_miles)

    p.check_deadlock()
    block = ctx.enter_context(nc.Block())
    p.replay_into(block)
    ctx.close()
    return nc


def _prelayout(inputs):
    f32 = np.float32
    Wq = np.asarray(inputs["L0_attn_a_Wq"], f32)
    Wk = np.asarray(inputs["L0_attn_a_Wk"], f32)
    Wv = np.asarray(inputs["L0_attn_a_Wv"], f32)
    Wo = np.asarray(inputs["L0_attn_a_Wo"], f32)
    wq = np.asarray(inputs["L0_attn_b_wq"], f32)
    wk = np.asarray(inputs["L0_attn_b_wk"], f32)
    wv = np.asarray(inputs["L0_attn_b_wv"], f32)
    wo = np.asarray(inputs["L0_attn_b_wo"], f32)
    up = np.asarray(inputs["L0_ffn_up"], f32)
    dn = np.asarray(inputs["L0_ffn_down"], f32)
    perm = np.asarray(inputs["L0_ffn_perm"]).astype(np.uint32)

    d = {}
    wcat = np.concatenate([Wq, Wk, Wv], axis=1)
    d["wqkv_a"] = np.ascontiguousarray(wcat.reshape(NCH, 128, 24, 256).transpose(2, 1, 0, 3))
    d["wo_a"] = np.ascontiguousarray(Wo.reshape(NCH, 128, 16, 256).transpose(2, 1, 0, 3))
    d["wq_b"] = np.ascontiguousarray(wq.reshape(NCH, 128, 8, 4, 128).transpose(2, 1, 0, 3, 4))
    wkvcat = np.concatenate([wk, wv], axis=1)
    d["wkv_b"] = np.ascontiguousarray(wkvcat.reshape(NCH, 128, 8, 256).transpose(2, 1, 0, 3))
    d["wo_b"] = np.ascontiguousarray(wo.reshape(8, 128, 16, 256).transpose(2, 1, 0, 3))
    d["wup"] = np.ascontiguousarray(up.reshape(16, 2, 128, 1024).transpose(0, 2, 1, 3))
    d["wdn"] = np.ascontiguousarray(dn.reshape(16, 8, 128, 256).transpose(0, 2, 1, 3))
    d["permc"] = np.ascontiguousarray(perm.reshape(32, 128).T)

    half = 64
    invf = 1.0 / 10000.0 ** (np.arange(half, dtype=np.float32) / np.float32(half))
    pos = np.arange(T, dtype=np.float32)
    ang = invf[:, None] * pos[None, :]
    cos = np.cos(ang).astype(f32)
    sin = np.sin(ang).astype(f32)
    cs1 = np.concatenate([cos, cos], axis=0)
    cs2 = np.concatenate([-sin, sin], axis=0)
    d["csq"] = (np.stack([cs1, cs2]) * np.float32(ISQ)).astype(f32)
    d["csk"] = np.stack([cs1, cs2]).astype(f32)

    ii, jj = np.indices((128, 128))
    d["maskc"] = np.where(jj <= ii, 0.0, -1e30).astype(f32)
    d["idenb"] = np.eye(128).astype(ml_dtypes.bfloat16)
    d["onescb"] = np.ones((128, 1), dtype=ml_dtypes.bfloat16)
    d["onesr"] = np.ones((1, 128), dtype=f32)
    d["epsc"] = np.full((128, 1), 1e-5, dtype=f32)
    d["bidx"] = np.tile(np.array([[0, 1]], dtype=np.uint32), (128, 1))
    return d


def kernel(L0_attn_a_X, L0_attn_a_Wq, L0_attn_a_Wk, L0_attn_a_Wv, L0_attn_a_Wo,
           L0_attn_b_wq, L0_attn_b_wk, L0_attn_b_wv, L0_attn_b_wo,
           L0_ffn_up, L0_ffn_down, L0_ffn_perm, n_sub=4, debug=False):
    inputs = dict(
        L0_attn_a_Wq=L0_attn_a_Wq, L0_attn_a_Wk=L0_attn_a_Wk,
        L0_attn_a_Wv=L0_attn_a_Wv, L0_attn_a_Wo=L0_attn_a_Wo,
        L0_attn_b_wq=L0_attn_b_wq, L0_attn_b_wk=L0_attn_b_wk,
        L0_attn_b_wv=L0_attn_b_wv, L0_attn_b_wo=L0_attn_b_wo,
        L0_ffn_up=L0_ffn_up, L0_ffn_down=L0_ffn_down, L0_ffn_perm=L0_ffn_perm,
    )
    shared = _prelayout(inputs)
    X = np.asarray(L0_attn_a_X, np.float32)
    nc = build_program(n_sub=n_sub, debug=debug)
    in_maps = []
    for c in range(8):
        xt = np.ascontiguousarray(X[c].T.reshape(D, NG, TG).transpose(1, 0, 2))
        m = dict(shared)
        m["xin"] = xt
        in_maps.append(m)
    res = run_bass_kernel_spmd(nc, in_maps, core_ids=list(range(8)))
    outs = []
    for c in range(8):
        xo = np.asarray(res.results[c]["xout"])
        outs.append(xo.transpose(1, 0, 2).reshape(D, T).T)
    out = np.ascontiguousarray(np.stack(outs).astype(np.float32))
    if debug:
        return out, res.results
    return out
